# version 22
# baseline (speedup 1.0000x reference)
import numpy as np
from contextlib import ExitStack
import concourse.bass as bass
import concourse.mybir as mybir
from concourse.bass_utils import run_bass_kernel_spmd

F32 = mybir.dt.float32
BF16 = mybir.dt.bfloat16
AF = mybir.ActivationFunctionType
ALU = mybir.AluOpType
AX = mybir.AxisListType

D, NCH, B, T, H = 1024, 8, 8, 2048, 16
OWN = 2048
NEG = -1.0e30
EPS = 1e-5


class P_:
    def __init__(self, nc, ctx):
        self.sem = {}
        self.cnt = {}
        for n in ("ld", "gp", "pe", "act", "dve", "pool", "st",
                  "w0", "w1", "w2", "w3", "v0", "v1", "qb", "rs",
                  "y0", "y1", "c0", "c1", "yp", "sg", "q0", "q1", "xb",
                  "x0", "x1", "x2", "x3"):
            self.sem[n] = ctx.enter_context(nc.semaphore(n))
            self.cnt[n] = 0
        self.tagged = set()
        self._keep = []

    def inc(self, instr, name, amt=None):
        if id(instr) in self.tagged:
            return
        amt = amt if amt is not None else 1
        instr.then_inc(self.sem[name], amt)
        self.cnt[name] += amt

    def wv(self, eng, name, val):
        if val > 0:
            eng.wait_ge(self.sem[name], val)

    def w(self, eng, *names):
        for n in names:
            if self.cnt[n] > 0:
                eng.wait_ge(self.sem[n], self.cnt[n])

    def wall(self, eng):
        self.w(eng, *self.sem.keys())


_PASSTHRU = ("wait_ge", "sem_inc", "nop", "dma_start", "dma_start_transpose",
             "indirect_dma_start", "trigger_dma", "collective_compute")


class SeqProxy:

    def __init__(self, eng, P, nm):
        self._eng, self._P, self._nm = eng, P, nm

    def __getattr__(self, name):
        f = getattr(self._eng, name)
        if name.startswith("_") or name in _PASSTHRU or not callable(f):
            return f

        def wrap(*args, **kw):
            P, nm = self._P, self._nm
            if P.cnt[nm] > 0:
                self._eng.wait_ge(P.sem[nm], P.cnt[nm])
            instr = f(*args, **kw)
            instr.then_inc(P.sem[nm], 1)
            P.cnt[nm] += 1
            P.tagged.add(id(instr))
            P._keep.append(instr)
            return instr
        return wrap


class BlkWrap:
    def __init__(self, blk, P):
        self._blk, self._P = blk, P

    @property
    def sync(self):
        return self._blk.sync

    @property
    def tensor(self):
        return self._blk.tensor

    @property
    def vector(self):
        return lambda fn: self._blk.vector(lambda e: fn(SeqProxy(e, self._P, "dve")))

    @property
    def scalar(self):
        return lambda fn: self._blk.scalar(lambda e: fn(SeqProxy(e, self._P, "act")))

    @property
    def gpsimd(self):
        return lambda fn: self._blk.gpsimd(lambda e: fn(SeqProxy(e, self._P, "pool")))


def build_nc(debug=False):
    nc = bass.Bass("TRN2", target_bir_lowering=False)

    d_xf = nc.dram_tensor("xf", [D, B * T], F32, kind="ExternalInput").ap()
    d_xo = nc.dram_tensor("xo", [D, OWN], F32, kind="ExternalInput").ap()
    d_wqkv = nc.dram_tensor("wqkv", [D, 3 * D], F32, kind="ExternalInput").ap()
    d_wout = nc.dram_tensor("wout", [D, D], F32, kind="ExternalInput").ap()
    d_w = {w: nc.dram_tensor(w, [D, D], F32, kind="ExternalInput").ap()
           for w in ("wf", "wi", "wc", "uf", "ui", "uc", "wo", "uo")}
    d_bv = {v: nc.dram_tensor(v, [128, NCH], F32, kind="ExternalInput").ap()
            for v in ("bfv", "biv", "bcv", "bov", "incv")}
    d_deny = nc.dram_tensor("deny", [128, 32], F32, kind="ExternalInput").ap()
    d_tri = nc.dram_tensor("trimask", [128, 128], F32, kind="ExternalInput").ap()
    d_iden = nc.dram_tensor("iden", [128, 128], F32, kind="ExternalInput").ap()
    d_yo = nc.dram_tensor("yo", [D, OWN], F32, kind="ExternalOutput").ap()

    dkind = "ExternalOutput" if debug else "Internal"
    d_wqkv_bf = nc.dram_tensor("wqkv_bf", [D, 3 * D], BF16, kind=dkind)
    d_wout_bf = nc.dram_tensor("wout_bf", [D, D], BF16, kind=dkind)
    d_wffn_bf = nc.dram_tensor("wffn_bf", [D, 8 * D], BF16, kind=dkind)
    d_qpark = nc.dram_tensor("q_park", [D, OWN], BF16, kind=dkind)
    d_ypark = nc.dram_tensor("y_park", [D, OWN], F32, kind=dkind)
    if debug:
        d_txh = nc.dram_tensor("t_xhat", [128, NCH * OWN], BF16, kind="ExternalOutput").ap()
        d_tkb = nc.dram_tensor("t_kb", [128, NCH * T], BF16, kind="ExternalOutput").ap()
        d_ton = nc.dram_tensor("t_onn", [128, NCH * 256], BF16, kind="ExternalOutput").ap()
        d_tor = nc.dram_tensor("t_onraw", [128, NCH * 256], BF16, kind="ExternalOutput").ap()
        d_trc = nc.dram_tensor("t_rec", [2, 2048], F32, kind="ExternalOutput").ap()
        d_tfa = nc.dram_tensor("t_fac", [128, NCH * 256], F32, kind="ExternalOutput").ap()
        d_tyt = nc.dram_tensor("t_yt", [128, 256], BF16, kind="ExternalOutput").ap()
        d_txr = nc.dram_tensor("t_xr", [128, 256], F32, kind="ExternalOutput").ap()
        d_tpy = nc.dram_tensor("t_py", [128, 256], BF16, kind="ExternalOutput").ap()
        d_twt = nc.dram_tensor("t_wt", [128, 128], F32, kind="ExternalOutput").ap()
        d_tnb = nc.dram_tensor("t_nb", [128, 1], F32, kind="ExternalOutput").ap()
        d_tz = nc.dram_tensor("t_z", [128, NCH * OWN], BF16, kind="ExternalOutput").ap()
        d_tb7 = nc.dram_tensor("t_b7", [128, 4 * 2048 + 512], F32, kind="ExternalOutput").ap()
    d_rowb = nc.dram_tensor("row_bounce", [1, T], F32)
    d_recb = nc.dram_tensor("rec_bounce", [2, 2048], F32)

    xf_r = d_xf.rearrange("(c p) n -> p c n", p=128)
    xo_r = d_xo.rearrange("(c p) n -> p c n", p=128)
    wqkv_r = d_wqkv.rearrange("(c p) f -> p c f", p=128)
    wout_r = d_wout.rearrange("(c p) f -> p c f", p=128)
    w_r = {k: v.rearrange("(c p) f -> p c f", p=128) for k, v in d_w.items()}
    qpark_r = d_qpark.ap().rearrange("(c p) n -> p c n", p=128)
    wqkvbf_r = d_wqkv_bf.ap().rearrange("(c p) f -> p c f", p=128)
    woutbf_r = d_wout_bf.ap().rearrange("(c p) f -> p c f", p=128)
    wffnbf_r = d_wffn_bf.ap().rearrange("(c p) f -> p c f", p=128)
    FFN_ORDER = ("wf", "wi", "wc", "uf", "ui", "uc", "wo", "uo")

    with ExitStack() as ctx:
        sb = lambda n, sh, dt: ctx.enter_context(nc.sbuf_tensor(n, sh, dt))
        tri = sb("tri", [128, 128], F32)
        iden = sb("iden_s", [128, 128], F32)
        deny = sb("deny_s", [128, 32], F32)
        ones_bf = sb("ones_bf", [128, 1], BF16)
        invd_bf = sb("invd_bf", [128, 1], BF16)
        ones_f1 = sb("ones_f1", [1, 128], F32)
        idy = sb("idy", [1, 1], F32)
        bv = {v: sb(v + "_s", [128, NCH], F32) for v in d_bv}
        ro_c = sb("ro_c", [128, 16], F32)
        lo_c = sb("lo_c", [128, 16], F32)
        co_c = sb("co_c", [128, 16], F32)
        qmax2 = sb("qmax2", [1, 1], F32)
        epsc = sb("epsc", [1, 1], F32)
        rw1 = sb("rw1", [1, T], F32)
        rw2 = sb("rw2", [1, T], F32)

        P = P_(nc, ctx)
        blk = BlkWrap(ctx.enter_context(nc.Block()), P)

        @blk.sync
        def _(s):
            P.inc(s.dma_start(out=tri[:], in_=d_tri), "ld", 16)
            P.inc(s.dma_start(out=iden[:], in_=d_iden), "ld", 16)
            P.inc(s.dma_start(out=deny[:], in_=d_deny), "ld", 16)
            for v in bv:
                P.inc(s.dma_start(out=bv[v][:], in_=d_bv[v]), "ld", 16)

        @blk.vector
        def _(v):
            v.memset(ones_bf[:], 1.0)
            v.memset(invd_bf[:], 1.0 / D)
            v.memset(ones_f1[:], 1.0)
            v.memset(idy[:], 1.0)
            v.memset(epsc[:], EPS)
            P.inc(v.memset(qmax2[:], 0.0), "dve")

        with ExitStack() as abctx:
            xa_bf = abctx.enter_context(nc.sbuf_tensor("xa_bf", [128, NCH, OWN], BF16))
            nmu_own_bf = abctx.enter_context(nc.sbuf_tensor("nmu_own_bf", [1, OWN], BF16))

            with ExitStack() as actx:
                asb = lambda n, sh, dt: actx.enter_context(nc.sbuf_tensor(n, sh, dt))
                x2sl = asb("x2sl_a", [128, NCH, 512], BF16)
                wq = asb("wq_a", [128, NCH, 1024], BF16)
                nwq = asb("nwq_a", [1, 1024], BF16)
                qev = asb("qev_a", [128, 2, 512], BF16)
                q2 = asb("q2_a", [128, 2, 512], BF16)
                rbc = asb("rbc_a", [128, 512], F32)
                qscr = asb("qscr_a", [1, 4], F32)

                wstage = asb("wstage_a", [128, NCH, 1024], BF16)

                xachk = []
                @blk.gpsimd
                def _(g):
                    for sl in range(4):
                        P.inc(g.dma_start(out=xa_bf[:, :, sl * 512:(sl + 1) * 512],
                                          in_=xo_r[:, :, sl * 512:(sl + 1) * 512]),
                              f"x{sl}", 16)
                        xachk.append(P.cnt[f"x{sl}"])
                    P.inc(g.dma_start(out=wq[:], in_=wqkv_r[:, :, 0:1024]), "gp", 16)
                for seg in range(4):
                    srcv = (wqkv_r[:, :, seg * 1024:(seg + 1) * 1024] if seg < 3
                            else wout_r[:, :, 0:1024])
                    dstv = (d_wqkv_bf.ap().rearrange("(c p) f -> p c f", p=128)[
                                :, :, seg * 1024:(seg + 1) * 1024] if seg < 3
                            else woutbf_r[:, :, 0:1024])
                    @blk.gpsimd
                    def _(g, srcv=srcv, seg=seg):
                        P.w(g, "st")
                        P.inc(g.dma_start(out=wstage[:], in_=srcv), "sg", 16)
                    sgc = P.cnt["sg"]
                    @blk.sync
                    def _(s, dstv=dstv, sgc=sgc):
                        P.wv(s, "sg", sgc)
                        P.inc(s.dma_start(out=dstv, in_=wstage[:]), "st", 16)

                with nc.psum_tensor("psA_su", [1, 2048], F32) as psu, \
                     nc.psum_tensor("psA_sq", [1, 2048], F32) as psq:
                    for sl in range(4):
                        @blk.vector
                        def _(v, sl=sl, lw=None):
                            P.wv(v, f"x{sl}", xachk[sl])
                            P.w(v, "pe")
                            v.tensor_mul(x2sl[:], xa_bf[:, :, sl * 512:(sl + 1) * 512],
                                         xa_bf[:, :, sl * 512:(sl + 1) * 512])
                        @blk.tensor
                        def _(t, sl=sl):
                            P.w(t, "dve")
                            for c in range(NCH):
                                t.matmul(psu[:, sl * 512:(sl + 1) * 512], invd_bf[:],
                                         xa_bf[:, c, sl * 512:(sl + 1) * 512],
                                         start=(c == 0), stop=(c == NCH - 1))
                                i = t.matmul(psq[:, sl * 512:(sl + 1) * 512], invd_bf[:],
                                             x2sl[:, c, :], start=(c == 0), stop=(c == NCH - 1))
                            P.inc(i, "pe")
                    @blk.vector
                    def _(v):
                        P.w(v, "pe")
                        v.tensor_copy(rw1[:], psu[:])
                        v.tensor_copy(rw2[:], psq[:])
                        v.tensor_scalar_mul(nmu_own_bf[:], rw1[:], -1.0)
                @blk.vector
                def _(v):
                    for sl in range(4):
                        v.tensor_mul(rbc[0:1, :], rw1[:, sl * 512:(sl + 1) * 512],
                                     rw1[:, sl * 512:(sl + 1) * 512])
                        v.tensor_sub(rw2[:, sl * 512:(sl + 1) * 512],
                                     rw2[:, sl * 512:(sl + 1) * 512], rbc[0:1, 0:512])
                @blk.scalar
                def _(sc):
                    P.w(sc, "dve")
                    sc.activation(rw2[:], rw2[:], AF.Sqrt, bias=epsc[:])
                with nc.psum_tensor("psA_tr", [128, 16], F32) as ptr:
                    @blk.tensor
                    def _(t):
                        P.w(t, "act")
                        for jt in range(16):
                            i = t.transpose(ptr[:, jt:jt + 1], rw2[:, jt * 128:(jt + 1) * 128],
                                            idy[:])
                        P.inc(i, "pe")
                    @blk.vector
                    def _(v):
                        P.w(v, "pe")
                        v.reciprocal(ro_c[:], ptr[:])
                        v.tensor_copy(co_c[:], ptr[:])
                        v.reciprocal(rw1[:], rw2[:])
                        v.tensor_scalar_mul(rw1[:], rw1[:], 0.125)
                @blk.scalar
                def _(sc):
                    P.w(sc, "dve")
                    sc.activation(lo_c[:], ro_c[:], AF.Ln)
                @blk.gpsimd
                def _(g):
                    P.w(g, "dve")
                    P.inc(g.dma_start(out=d_rowb.ap(), in_=rw1[:]), "gp", 16)
                with nc.psum_tensor("psA_ws", [1, 128], F32) as pws:
                    for fc in range(8):
                        @blk.tensor
                        def _(t, fc=fc):
                            P.w(t, "gp")
                            if fc:
                                P.w(t, "dve")
                            for ec in range(NCH):
                                i = t.matmul(pws[:], ones_bf[:],
                                             wq[:, ec, fc * 128:(fc + 1) * 128],
                                             start=(ec == 0), stop=(ec == NCH - 1))
                            P.inc(i, "pe")
                        @blk.vector
                        def _(v, fc=fc):
                            P.w(v, "pe")
                            v.tensor_copy(nwq[:, fc * 128:(fc + 1) * 128], pws[:])
                nwqdve = P.cnt["dve"]
                pqevac = [P.cnt["dve"], P.cnt["dve"]]
                qparkst = [0, 0]
                rbcgp = 0
                with nc.psum_tensor("psA_q0", [128, 512], F32) as pq0, \
                     nc.psum_tensor("psA_q1", [128, 512], F32) as pq1, \
                     nc.psum_tensor("psA_n0", [1, 512], F32) as pn0, \
                     nc.psum_tensor("psA_n1", [1, 512], F32) as pn1:
                    pqs = (pq0, pq1)
                    for sl in range(4):
                        @blk.gpsimd
                        def _(g, sl=sl, w0=pqevac[0], w1=pqevac[1]):
                            P.w(g, "gp")
                            P.wv(g, "dve", max(w0, w1))
                            P.inc(g.dma_start(
                                out=rbc[:],
                                in_=d_rowb.ap().partition_broadcast(128)[:, 0, sl * 512:(sl + 1) * 512]),
                                "rs", 16)
                        rbcgp = P.cnt["rs"]
                        for fc in range(8):
                            k = fc % 2
                            pq = pqs[k]
                            @blk.tensor
                            def _(t, fc=fc, sl=sl, pq=pq, dw=max(nwqdve, pqevac[k]),
                                  gw=rbcgp):
                                P.wv(t, "rs", gw)
                                P.wv(t, "dve", dw)
                                for c in range(NCH):
                                    t.matmul(pq[:], wq[:, c, fc * 128:(fc + 1) * 128],
                                             xa_bf[:, c, sl * 512:(sl + 1) * 512],
                                             start=(c == 0), stop=False)
                                i = t.matmul(pq[:], nwq[:, fc * 128:(fc + 1) * 128],
                                             nmu_own_bf[:, sl * 512:(sl + 1) * 512],
                                             start=False, stop=True)
                                P.inc(i, "pe")
                            qch = P.cnt["pe"]
                            @blk.vector
                            def _(v, k=k, pq=pq, qch=qch, sw=qparkst[k]):
                                P.wv(v, "pe", qch)
                                P.wv(v, f"q{k}", sw)
                                v.tensor_mul(qev[:, k, :], pq[:], rbc[:])
                                v.tensor_mul(q2[:, k, :], qev[:, k, :], qev[:, k, :])
                            pqevac[k] = P.cnt["dve"]
                            @blk.sync
                            def _(s, fc=fc, sl=sl, k=k, dw=pqevac[k]):
                                P.wv(s, "dve", dw)
                                P.inc(s.dma_start(
                                    out=d_qpark.ap()[fc * 128:(fc + 1) * 128,
                                                     sl * 512:(sl + 1) * 512],
                                    in_=qev[:, k, :]), f"q{k}", 16)
                            qparkst[k] = P.cnt[f"q{k}"]
                            @blk.tensor
                            def _(t, k=k, dw=pqevac[k]):
                                P.wv(t, "dve", dw)
                                t.matmul(pn0[:], ones_bf[0:64, :], q2[0:64, k, :],
                                         start=True, stop=True)
                                P.inc(t.matmul(pn1[:], ones_bf[64:128, :], q2[64:128, k, :],
                                               start=True, stop=True), "pe")
                            pnch = P.cnt["pe"]
                            @blk.vector
                            def _(v, pnch=pnch):
                                P.wv(v, "pe", pnch)
                                v.tensor_reduce(qscr[:, 0:1], pn0[:], axis=AX.X, op=ALU.max)
                                v.tensor_reduce(qscr[:, 1:2], pn1[:], axis=AX.X, op=ALU.max)
                                v.tensor_max(qscr[:, 0:1], qscr[:, 0:1], qscr[:, 1:2])
                                v.tensor_max(qmax2[:], qmax2[:], qscr[:, 0:1])
            for dec in (blk.sync, blk.gpsimd, blk.tensor, blk.scalar, blk.vector):
                @dec
                def _(e):
                    P.wall(e)
            @blk.scalar
            def _(sc):
                P.w(sc, "dve")
                P.inc(sc.activation(qmax2[:], qmax2[:], AF.Sqrt), "act")
            if debug:
                @blk.sync
                def _(s):
                    P.inc(s.dma_start(out=d_txh, in_=xa_bf[:].rearrange("p c n -> p (c n)")),
                          "st", 16)
                    s.wait_ge(P.sem["st"], P.cnt["st"])

            with ExitStack() as bctx:
                bsb = lambda n, sh, dt: bctx.enter_context(nc.sbuf_tensor(n, sh, dt))
                xb_bf = bsb("xb_bf", [128, NCH, T], BF16)
                kb = bsb("kb", [128, NCH, T], BF16)
                vb = bsb("vb", [128, 16, 1040], BF16)
                kob = bsb("kob", [128, NCH, 256], BF16)
                vob = bsb("vob", [128, 2, 1040], BF16)
                nwv = bsb("nwv", [1, 256], BF16)
                nwk = bsb("nwk", [1, 128], BF16)
                nmu_b_bf = bsb("nmu_b_bf", [1, T], BF16)
                wt = bsb("wt", [128, 4, NCH, 128], BF16)
                wvres = bsb("wvres", [128, 2, NCH, 256], BF16)
                qb = bsb("qb", [128, NCH, 256], BF16)
                scr8 = bsb("scr8", [128, NCH, 256], F32)
                yt8 = bsb("yt8", [128, 2, 256], F32)
                r_cb = bsb("r_cb", [128, 16], F32)
                l_cb = bsb("l_cb", [128, 16], F32)
                c_cb = bsb("c_cb", [128, 16], F32)
                btab = bsb("btab", [128, 32], F32)
                btabd = bsb("btabd", [128, 2], F32)
                negb = bsb("negb", [128, 1], F32)
                bscr = bsb("bscr", [1, 8], F32)


                wstage_b = bsb("wstage_b", [128, NCH, 256], BF16)
                maxkf = bsb("maxkf", [128, 32], F32)
                wtuse = [0, 0, 0, 0]
                wtld = [0, 0, 0, 0]
                wvuse = [0, 0]
                wvld = [0, 0]
                wold = [0, 0, 0, 0]
                x2use = [0, 0, 0, 0]
                qbfree = [0]
                qbld = [0]
                stage_st = [0]
                kpre = [False]

                def emit_B1(bb, xbgp):
                    x2v = [kb[:, j, :].rearrange("p (c n) -> p c n", c=8)
                           for j in range(4)]
                    pe_entry = P.cnt["pe"]
                    for j in range(4):
                        x2use[j] = max(x2use[j], pe_entry)
                    with nc.psum_tensor(f"psB_su{bb}", [1, 2048], F32) as psu, \
                         nc.psum_tensor(f"psB_sq{bb}", [1, 2048], F32) as psq:
                        for sl in range(8):
                            k = sl % 4
                            @blk.vector
                            def _(v, sl=sl, k=k, pw=x2use[k], xbgp=xbgp):
                                P.wv(v, "xb", xbgp)
                                P.wv(v, "pe", pw)
                                v.tensor_mul(x2v[k], xb_bf[:, :, sl * 256:(sl + 1) * 256],
                                             xb_bf[:, :, sl * 256:(sl + 1) * 256])
                            sq = P.cnt["dve"]
                            @blk.tensor
                            def _(t, sl=sl, k=k, sq=sq, xbgp=xbgp):
                                P.wv(t, "dve", sq)
                                P.wv(t, "xb", xbgp)
                                for c in range(NCH):
                                    t.matmul(psu[:, sl * 256:(sl + 1) * 256], invd_bf[:],
                                             xb_bf[:, c, sl * 256:(sl + 1) * 256],
                                             start=(c == 0), stop=(c == NCH - 1))
                                    i = t.matmul(psq[:, sl * 256:(sl + 1) * 256], invd_bf[:],
                                                 x2v[k][:, c, :],
                                                 start=(c == 0), stop=(c == NCH - 1))
                                P.inc(i, "pe")
                            x2use[k] = P.cnt["pe"]
                        @blk.vector
                        def _(v):
                            P.w(v, "pe", "gp")
                            v.tensor_copy(rw1[:], psu[:])
                            v.tensor_copy(rw2[:], psq[:])
                            v.tensor_scalar_mul(nmu_b_bf[:], rw1[:], -1.0)

                def emit_B1_tail(bb):
                    with nc.psum_tensor(f"psB_var{bb}", [1, 512], F32) as pvar:
                        @blk.vector
                        def _(v):
                            for sl in range(4):
                                v.tensor_mul(pvar[:], rw1[:, sl * 512:(sl + 1) * 512],
                                             rw1[:, sl * 512:(sl + 1) * 512])
                                v.tensor_sub(rw2[:, sl * 512:(sl + 1) * 512],
                                             rw2[:, sl * 512:(sl + 1) * 512], pvar[:])
                    @blk.scalar
                    def _(sc):
                        P.w(sc, "dve")
                        sc.activation(rw2[:], rw2[:], AF.Sqrt, bias=epsc[:])
                    with nc.psum_tensor(f"psB_tr{bb}", [128, 16], F32) as ptrb:
                        @blk.tensor
                        def _(t):
                            P.w(t, "act")
                            for jt in range(16):
                                i = t.transpose(ptrb[:, jt:jt + 1],
                                                rw2[:, jt * 128:(jt + 1) * 128], idy[:])
                            P.inc(i, "pe")
                        @blk.vector
                        def _(v):
                            P.w(v, "pe")
                            v.reciprocal(r_cb[:], ptrb[:])
                            v.tensor_copy(c_cb[:], ptrb[:])
                            v.tensor_reduce(bscr[:, 1:2], rw2[:], axis=AX.X, op=ALU.min)
                            v.reciprocal(bscr[:, 1:2], bscr[:, 1:2])
                    @blk.scalar
                    def _(sc):
                        P.w(sc, "dve")
                        sc.activation(l_cb[:], r_cb[:], AF.Ln)
                        sc.activation(bscr[:, 2:3], bscr[:, 1:2], AF.Ln)
                    tailact[0] = P.cnt["act"]

                qpark_done = [(0, 0)]
                tailact = [0]
                xbgp_l = [0]
                ystor = [0]
                for b in range(B):
                    if b == 0:
                        @blk.gpsimd
                        def _(g):
                            P.w(g, "pe", "dve")
                            P.inc(g.dma_start(out=xb_bf[:],
                                              in_=xf_r[:, :, 0:T]), "xb", 16)
                        xbgp_l[0] = P.cnt["xb"]
                        qpark_done[0] = (P.cnt["q0"], P.cnt["q1"])
                        emit_B1(0, xbgp_l[0])
                        emit_B1_tail(0)
                        for slot in range(4):
                            @blk.sync
                            def _(s, slot=slot):
                                P.inc(s.dma_start(out=wt[:, slot],
                                                  in_=wqkvbf_r[:, :, 1024 + slot * 128:
                                                               1024 + (slot + 1) * 128]),
                                      f"w{slot}", 16)
                            wtld[slot] = P.cnt[f"w{slot}"]
                        for ft in range(2):
                            @blk.sync
                            def _(s, ft=ft):
                                P.inc(s.dma_start(out=wvres[:, ft],
                                                  in_=wqkvbf_r[:, :, 2048 + ft * 256:
                                                               2048 + (ft + 1) * 256]),
                                      f"v{ft}", 16)
                            wvld[ft] = P.cnt[f"v{ft}"]
                    @blk.sync
                    def _(s, b=b, pw=qbfree[0], sw=qpark_done[0]):
                        P.wv(s, "q0", sw[0])
                        P.wv(s, "q1", sw[1])
                        P.wv(s, "pe", pw)
                        P.inc(s.dma_start(out=qb[:],
                                          in_=qpark_r[:, :, b * 256:(b + 1) * 256]), "qb", 16)
                    qbld[0] = P.cnt["qb"]
                    nwkrdy = [P.cnt["dve"]]
                    pkevac = [P.cnt["dve"], P.cnt["dve"]]
                    pkoevac = [P.cnt["dve"], P.cnt["dve"]]
                    with nc.psum_tensor(f"psB_kA_{b}", [128, 512], F32) as pkA, \
                         nc.psum_tensor(f"psB_kB_{b}", [128, 512], F32) as pkB, \
                         nc.psum_tensor(f"psB_koA_{b}", [128, 256], F32) as pkoA, \
                         nc.psum_tensor(f"psB_koB_{b}", [128, 256], F32) as pkoB:
                        pks = (pkA, pkB)
                        pkos = (pkoA, pkoB)
                        for fi in range(8):
                            slot = fi % 4
                            with nc.psum_tensor(f"psB_wk{b}_{fi}", [1, 128], F32) as pws2:
                                @blk.tensor
                                def _(t, slot=slot, pws2=pws2, lw=wtld[slot], dw=nwkrdy[0]):
                                    P.wv(t, f"w{slot}", lw)
                                    P.wv(t, "dve", dw)
                                    for ec in range(NCH):
                                        i = t.matmul(pws2[:], ones_bf[:], wt[:, slot, ec, :],
                                                     start=(ec == 0), stop=(ec == NCH - 1))
                                    P.inc(i, "pe")
                                wsc = P.cnt["pe"]
                                @blk.vector
                                def _(v, pws2=pws2, wsc=wsc):
                                    P.wv(v, "pe", wsc)
                                    v.tensor_copy(nwk[:], pws2[:])
                                nwkrdy[0] = P.cnt["dve"]
                            for tt in range(4):
                                pk = pks[tt % 2]
                                @blk.tensor
                                def _(t, tt=tt, pk=pk, slot=slot,
                                      dw=max(pkevac[tt % 2], nwkrdy[0]), lw=wtld[slot]):
                                    P.wv(t, f"w{slot}", lw)
                                    P.wv(t, "dve", dw)
                                    for c in range(NCH):
                                        t.matmul(pk[:], wt[:, slot, c, :],
                                                 xb_bf[:, c, tt * 512:(tt + 1) * 512],
                                                 start=(c == 0), stop=False)
                                    i = t.matmul(pk[:], nwk[:],
                                                 nmu_b_bf[:, tt * 512:(tt + 1) * 512],
                                                 start=False, stop=True)
                                    P.inc(i, "pe")
                                kch = P.cnt["pe"]
                                @blk.vector
                                def _(v, fi=fi, tt=tt, pk=pk, kch=kch):
                                    P.wv(v, "pe", kch)
                                    v.tensor_copy(kb[:, fi, tt * 512:(tt + 1) * 512], pk[:])
                                    v.tensor_reduce(maxkf[:, fi * 4 + tt:fi * 4 + tt + 1],
                                                    pk[:], axis=AX.X, op=ALU.max,
                                                    apply_absolute_value=True)
                                pkevac[tt % 2] = P.cnt["dve"]
                            pko = pkos[fi % 2]
                            @blk.tensor
                            def _(t, b=b, pko=pko, slot=slot,
                                  dw=max(pkoevac[fi % 2], nwkrdy[0])):
                                P.wv(t, "dve", dw)
                                for c in range(NCH):
                                    t.matmul(pko[:], wt[:, slot, c, :],
                                             xa_bf[:, c, b * 256:(b + 1) * 256],
                                             start=(c == 0), stop=False)
                                i = t.matmul(pko[:], nwk[:],
                                             nmu_own_bf[:, b * 256:(b + 1) * 256],
                                             start=False, stop=True)
                                P.inc(i, "pe")
                            koch = P.cnt["pe"]
                            @blk.vector
                            def _(v, fi=fi, pko=pko, koch=koch):
                                P.wv(v, "pe", koch)
                                v.tensor_copy(kob[:, fi, :], pko[:])
                            pkoevac[fi % 2] = P.cnt["dve"]
                            wtuse[slot] = P.cnt["pe"]
                            if fi < 4:
                                @blk.sync
                                def _(s, fi=fi, slot=slot, pw=wtuse[slot]):
                                    P.wv(s, "pe", pw)
                                    P.inc(s.dma_start(out=wt[:, slot],
                                                      in_=wqkvbf_r[:, :, 1024 + (fi + 4) * 128:
                                                                   1024 + (fi + 5) * 128]),
                                          f"w{slot}", 16)
                                wtld[slot] = P.cnt[f"w{slot}"]
                    if b > 0:
                        emit_B1_tail(b)
                    nwvrdy = [P.cnt["dve"]]
                    pvevac = [P.cnt["dve"], P.cnt["dve"]]
                    with nc.psum_tensor(f"psB_v0_{b}", [128, 256], F32) as pv0, \
                         nc.psum_tensor(f"psB_v1_{b}", [128, 256], F32) as pv1, \
                         nc.psum_tensor(f"psB_wsv{b}", [1, 256], F32) as pwsv:
                        pvs = (pv0, pv1)
                        for ft in range(4):
                            slot = ft % 2
                            @blk.tensor
                            def _(t, slot=slot, lw=wvld[slot], dw=nwvrdy[0]):
                                P.wv(t, f"v{slot}", lw)
                                P.wv(t, "dve", dw)
                                for ec in range(NCH):
                                    i = t.matmul(pwsv[:], ones_bf[:], wvres[:, slot, ec, :],
                                                 start=(ec == 0), stop=(ec == NCH - 1))
                                P.inc(i, "pe")
                            wsc2 = P.cnt["pe"]
                            @blk.vector
                            def _(v, wsc2=wsc2):
                                P.wv(v, "pe", wsc2)
                                v.tensor_copy(nwv[:], pwsv[:])
                            nwvrdy[0] = P.cnt["dve"]
                            for jt in range(16):
                                pv = pvs[jt % 2]
                                @blk.tensor
                                def _(t, jt=jt, slot=slot, pv=pv,
                                      dw=max(pvevac[jt % 2], nwvrdy[0])):
                                    P.wv(t, "dve", dw)
                                    for ec in range(NCH):
                                        t.matmul(pv[:],
                                                 xb_bf[:, ec, jt * 128:(jt + 1) * 128],
                                                 wvres[:, slot, ec, :],
                                                 start=(ec == 0), stop=False)
                                    i = t.matmul(pv[:],
                                                 nmu_b_bf[:, jt * 128:(jt + 1) * 128],
                                                 nwv[:],
                                                 start=False, stop=True)
                                    P.inc(i, "pe")
                                vch = P.cnt["pe"]
                                @blk.vector
                                def _(v, jt=jt, ft=ft, pv=pv, vch=vch):
                                    P.wv(v, "pe", vch)
                                    dst = vb[:].rearrange("p jt (h e) -> p jt h e", h=16)[
                                        :, jt, ft * 4:(ft + 1) * 4, 0:64]
                                    v.tensor_copy(dst, pv[:].rearrange("p (h e) -> p h e", h=4))
                                pvevac[jt % 2] = P.cnt["dve"]
                            for sj in range(2):
                                pv = pvs[sj]
                                @blk.tensor
                                def _(t, sj=sj, slot=slot, b=b, pv=pv,
                                      dw=max(pvevac[sj], nwvrdy[0])):
                                    P.wv(t, "dve", dw)
                                    for ec in range(NCH):
                                        t.matmul(pv[:],
                                                 xa_bf[:, ec, b * 256 + sj * 128:
                                                        b * 256 + (sj + 1) * 128],
                                                 wvres[:, slot, ec, :],
                                                 start=(ec == 0), stop=False)
                                    i = t.matmul(pv[:],
                                                 nmu_own_bf[:, b * 256 + sj * 128:
                                                            b * 256 + (sj + 1) * 128],
                                                 nwv[:],
                                                 start=False, stop=True)
                                    P.inc(i, "pe")
                                vch = P.cnt["pe"]
                                @blk.vector
                                def _(v, sj=sj, ft=ft, pv=pv, vch=vch):
                                    P.wv(v, "pe", vch)
                                    dst = vob[:].rearrange("p jt (h e) -> p jt h e", h=16)[
                                        :, sj, ft * 4:(ft + 1) * 4, 0:64]
                                    v.tensor_copy(dst, pv[:].rearrange("p (h e) -> p h e", h=4))
                                pvevac[sj] = P.cnt["dve"]
                            wvuse[slot] = P.cnt["pe"]
                            if ft < 2:
                                @blk.sync
                                def _(s, ft=ft, slot=slot, pw=wvuse[slot]):
                                    P.wv(s, "pe", pw)
                                    P.inc(s.dma_start(
                                        out=wvres[:, slot],
                                        in_=wqkvbf_r[:, :, 2048 + (ft + 2) * 256:
                                                     2048 + (ft + 3) * 256]), f"v{slot}", 16)
                                wvld[slot] = P.cnt[f"v{slot}"]
                        for jt in range(16):
                            @blk.vector
                            def _(v, jt=jt):
                                dst = vb[:].rearrange("p jt (h e) -> p jt h e", h=16)[:, jt, :, 64:65]
                                src2 = bass.AP(c_cb, jt, [list(c_cb.ap().ap[0]), [0, 16], [1, 1]])
                                v.tensor_copy(dst, src2)
                        for sj in range(2):
                            @blk.vector
                            def _(v, sj=sj, b=b):
                                dst = vob[:].rearrange("p jt (h e) -> p jt h e", h=16)[:, sj, :, 64:65]
                                src2 = bass.AP(co_c, b * 2 + sj, [list(co_c.ap().ap[0]), [0, 16], [1, 1]])
                                v.tensor_copy(dst, src2)

                    if b + 1 < B:
                        @blk.gpsimd
                        def _(g, b=b):
                            P.w(g, "pe", "dve")
                            P.inc(g.dma_start(out=xb_bf[:],
                                              in_=xf_r[:, :, (b + 1) * T:(b + 2) * T]),
                                  "xb", 16)
                        xbgp_l[0] = P.cnt["xb"]
                    @blk.vector
                    def _(v):
                        P.inc(v.tensor_reduce(
                            negb[:], maxkf[:],
                            axis=AX.X, op=ALU.max, apply_absolute_value=True), "dve")
                    with nc.psum_tensor(f"psB_km{b}", [1, 128], F32) as pkm, \
                         nc.psum_tensor(f"psB_nb{b}", [128, 1], F32) as pnb:
                        @blk.tensor
                        def _(t, pkm=pkm):
                            P.w(t, "dve")
                            P.inc(t.transpose(pkm[:], negb[:], iden[:]), "pe")
                        @blk.vector
                        def _(v, pkm=pkm):
                            P.w(v, "pe")
                            v.tensor_reduce(bscr[:, 0:1], pkm[:], axis=AX.X, op=ALU.max)
                            P.inc(v.tensor_mul(bscr[:, 0:1], bscr[:, 0:1], bscr[:, 1:2]), "dve")
                        @blk.vector
                        def _(v, aw=None):
                            P.wv(v, "act", tailact[0])
                            v.tensor_mul(bscr[:, 0:1], bscr[:, 0:1], qmax2[:])
                            v.tensor_scalar_mul(bscr[:, 0:1], bscr[:, 0:1], 8.0)
                            v.tensor_add(bscr[:, 0:1], bscr[:, 0:1], bscr[:, 2:3])
                            P.inc(v.tensor_scalar_mul(bscr[:, 0:1], bscr[:, 0:1], -1.0), "dve")
                        @blk.tensor
                        def _(t, pnb=pnb):
                            P.w(t, "dve")
                            P.inc(t.matmul(pnb[:], ones_f1[:], bscr[:, 0:1],
                                           start=True, stop=True), "pe")
                        @blk.vector
                        def _(v, pnb=pnb):
                            P.w(v, "pe")
                            P.inc(v.tensor_copy(negb[:], pnb[:]), "dve")
                    @blk.vector
                    def _(v, b=b, aw=tailact[0]):
                        P.w(v, "dve")
                        P.wv(v, "act", aw)
                        lview = bass.AP(l_cb, 0, [list(l_cb.ap().ap[0]), [0, 2], [1, 16]])
                        v.tensor_add(btab[:].rearrange("p (s k) -> p s k", s=2), lview,
                                     deny[:].rearrange("p (s k) -> p s k", s=2))
                        v.tensor_scalar_add(btab[:], btab[:], negb[:])
                        v.tensor_copy(btabd[:], lo_c[:, 2 * b:2 * b + 2])
                        P.inc(v.tensor_scalar_add(btabd[:], btabd[:], negb[:]), "dve")
                    @blk.tensor
                    def _(t, lw=qbld[0]):
                        P.wv(t, "qb", lw)
                    @blk.scalar
                    def _(sc):
                        P.w(sc, "dve", "pool")


                    with ExitStack() as sctx:
                        ssb = lambda n, sh, dt: sctx.enter_context(nc.sbuf_tensor(n, sh, dt))
                        psb = [ssb(f"psb{se}_{b}", [128, 1024], BF16) for se in range(2)]
                        onn = ssb(f"onn_{b}", [128, NCH, 256], BF16)
                        rec0, rec1 = rw1, rw2
                        recs = (rec0, rec1)
                        with nc.psum_tensor(f"psB_scA_{b}", [128, 1024], F32) as scA, \
                             nc.psum_tensor(f"psB_scB_{b}", [128, 1024], F32) as scB, \
                             nc.psum_tensor(f"psB_av00_{b}", [128, 512], F32) as av00, \
                             nc.psum_tensor(f"psB_av01_{b}", [128, 512], F32) as av01, \
                             nc.psum_tensor(f"psB_av10_{b}", [128, 512], F32) as av10, \
                             nc.psum_tensor(f"psB_av11_{b}", [128, 512], F32) as av11:
                            sc = (scA, scB)
                            av = ((av00, av01), (av10, av11))
                            expcnt = [0, 0]
                            avevcnt = [(P.cnt["dve"], P.cnt["pool"]),
                                       (P.cnt["dve"], P.cnt["pool"])]
                            tricnt = [P.cnt["dve"]]
                            pending = [None]
                            slot = 0
                            for hpg in range(2):
                                hps = [4 * hpg + i_ for i_ in range(4)]
                                for s_ in (1, 0):
                                    g2 = slot % 2
                                    units = [(True, 0)] + [(False, k) for k in range(8 * (s_ + 1))]
                                    nu = len(units)
                                    pe_base = P.cnt["pe"]
                                    act_base = P.cnt["act"]
                                    dve_tri = P.cnt["dve"] + 1
                                    @blk.scalar
                                    def _(scl_, units=units, s_=s_, b=b, pe_base=pe_base,
                                          dve_tri=dve_tri):
                                        for u, (isdiag, k) in enumerate(units):
                                            p2, r2 = divmod(u, 2)
                                            P.wv(scl_, "pe", pe_base + 4 * p2 + 1 + r2)
                                            if isdiag:
                                                P.wv(scl_, "dve", dve_tri)
                                                scalev = ro_c[:, 2 * b + s_: 2 * b + s_ + 1]
                                                bia = btabd[:, s_:s_ + 1]
                                            else:
                                                scalev = r_cb[:, k:k + 1]
                                                bia = btab[:, s_ * 16 + k: s_ * 16 + k + 1]
                                            se = u % 2
                                            scl_.activation(psb[se][:], sc[se][:],
                                                            AF.Exp, bias=bia,
                                                            scale=scalev)
                                    assert P.cnt["act"] == act_base + nu
                                    ui = 0
                                    first_pair = True
                                    while ui < nu:
                                        pu = [(ui + j, units[ui + j][0], units[ui + j][1],
                                               (ui + j) % 2) for j in range(min(2, nu - ui))]
                                        for (uj, isdiag, k, se) in pu:
                                            @blk.tensor
                                            def _(t, s_=s_, isdiag=isdiag, k=k, uj=uj, se=se,
                                                  hps=hps):
                                                P.wv(t, "act", expcnt[se])
                                                if uj == 0:
                                                    P.wv(t, "dve", tricnt[0])
                                                for hi, hp in enumerate(hps):
                                                    lt = (kob[:, hp, s_ * 128:(s_ + 1) * 128]
                                                          if isdiag
                                                          else kb[:, hp, k * 128:(k + 1) * 128])
                                                    t.matmul(sc[se][:, hi * 128:(hi + 1) * 128],
                                                             lt[0:64, :],
                                                             qb[0:64, hp, s_ * 128:(s_ + 1) * 128],
                                                             start=True, stop=True,
                                                             tile_position=(0, 0))
                                                    i = t.matmul(
                                                        sc[se][:, 512 + hi * 128:512 + (hi + 1) * 128],
                                                        lt[64:128, :],
                                                        qb[64:128, hp, s_ * 128:(s_ + 1) * 128],
                                                        start=True, stop=True,
                                                        tile_position=(64, 0))
                                                P.inc(i, "pe")
                                            assert P.cnt["pe"] == pe_base + 4 * (uj // 2) + 1 + uj % 2
                                            if isdiag:
                                                @blk.vector
                                                def _(v, se=se):
                                                    P.w(v, "pe")
                                                    trib8 = bass.AP(tri, 0,
                                                                    [list(tri.ap().ap[0]),
                                                                     [0, 8], [1, 128]])
                                                    v.tensor_add(
                                                        sc[se][:].rearrange(
                                                            "p (hp i) -> p hp i", hp=8),
                                                        sc[se][:].rearrange(
                                                            "p (hp i) -> p hp i", hp=8),
                                                        trib8)
                                                tricnt[0] = P.cnt["dve"]
                                                assert tricnt[0] == dve_tri
                                        for (uj, isdiag, k, se) in pu:
                                            @blk.tensor
                                            def _(t, s_=s_, isdiag=isdiag, k=k, uj=uj, se=se,
                                                  hps=hps, g2=g2,
                                                  ecnt=act_base + (uj + 1),
                                                  last_u=(uj == nu - 1)):
                                                P.wv(t, "act", ecnt)
                                                if uj == 0:
                                                    P.wv(t, "dve", avevcnt[g2][0])
                                                    P.wv(t, "pool", avevcnt[g2][1])
                                                for h01 in range(2):
                                                    for hi, hp in enumerate(hps):
                                                        h = 2 * hp + h01
                                                        vt = (vob[:, s_, h * 65:(h + 1) * 65]
                                                              if isdiag
                                                              else vb[:, k, h * 65:(h + 1) * 65])
                                                        i = t.matmul(
                                                            av[g2][h01][0:65,
                                                                        hi * 128:(hi + 1) * 128],
                                                            vt,
                                                            psb[se][:, h01 * 512 + hi * 128:
                                                                    h01 * 512 + (hi + 1) * 128],
                                                            start=(uj == 0 and hi == 0),
                                                            stop=last_u)
                                                P.inc(i, "pe")
                                        ui += len(pu)
                                        if first_pair:
                                            first_pair = False
                                            if pending[0] is not None:
                                                pending[0]()
                                                pending[0] = None
                                    last_even = nu - 1 if (nu - 1) % 2 == 0 else nu - 2
                                    last_odd = nu - 1 if (nu - 1) % 2 == 1 else nu - 2
                                    expcnt[0] = act_base + (last_even + 1)
                                    expcnt[1] = act_base + (last_odd + 1)
                                    def _evac(s_=s_, hpg=hpg, g2=g2):
                                        @blk.vector
                                        def _(v):
                                            P.w(v, "pe")
                                            for h01 in range(2):
                                                dst = onn[h01 * 64:(h01 + 1) * 64,
                                                            4 * hpg:4 * hpg + 4,
                                                            s_ * 128:(s_ + 1) * 128]
                                                v.tensor_copy(dst,
                                                              av[g2][h01][0:64, :].rearrange(
                                                                  "p (c n) -> p c n", c=4))
                                        @blk.vector
                                        def _(v):
                                            for h01 in range(2):
                                                rdst = bass.AP(recs[h01],
                                                               4 * hpg * 256 + s_ * 128,
                                                               [list(recs[h01].ap().ap[0]),
                                                                [256, 4], [1, 128]])
                                                v.tensor_copy(rdst, av[g2][h01][64:65, :]
                                                              .rearrange("p (c n) -> p c n",
                                                                         c=4))
                                        avevcnt[g2] = (P.cnt["dve"], P.cnt["pool"])
                                        @blk.gpsimd
                                        def _(g):
                                            P.w(g, "dve")
                                            for h01 in range(2):
                                                wdst = d_recb.ap()[h01:h01 + 1, :].rearrange(
                                                    "a (c u n) -> a c u n", c=8, u=2)[
                                                    :, 4 * hpg:4 * hpg + 4, s_, :]
                                                wsrc = bass.AP(recs[h01],
                                                               4 * hpg * 256 + s_ * 128,
                                                               [list(recs[h01].ap().ap[0]),
                                                                [256, 4], [1, 128]])
                                                P.inc(g.dma_start(out=wdst, in_=wsrc),
                                                      "gp", 16)
                                            g.wait_ge(P.sem["gp"], P.cnt["gp"])
                                            for h01 in range(2):
                                                bsrc = bass.AP(
                                                    d_recb,
                                                    h01 * 2048 + 4 * hpg * 256 + s_ * 128,
                                                    [[0, 64], [256, 4], [1, 128]])
                                                bdst = scr8[h01 * 64:(h01 + 1) * 64,
                                                            4 * hpg:4 * hpg + 4,
                                                            s_ * 128:(s_ + 1) * 128]
                                                P.inc(g.dma_start(out=bdst, in_=bsrc),
                                                      "gp", 16)
                                        @blk.vector
                                        def _(v):
                                            P.w(v, "gp")
                                            for h01 in range(2):
                                                sv = scr8[h01 * 64:(h01 + 1) * 64,
                                                          4 * hpg:4 * hpg + 4,
                                                          s_ * 128:(s_ + 1) * 128]
                                                v.reciprocal(sv, sv)
                                                onnv = onn[h01 * 64:(h01 + 1) * 64,
                                                           4 * hpg:4 * hpg + 4,
                                                           s_ * 128:(s_ + 1) * 128]
                                                v.tensor_mul(onnv, onnv, sv)
                                    pending[0] = _evac
                                    slot += 1
                            if pending[0] is not None:
                                pending[0]()
                                pending[0] = None
                        actB6 = P.cnt["act"]
                        qbfree[0] = P.cnt["pe"]
                        factor_gp = P.cnt["gp"]
                        for fc in range(4):
                            @blk.sync
                            def _(s, fc=fc, pw=wtuse[fc]):
                                P.wv(s, "pe", pw)
                                P.inc(s.dma_start(out=wt[:, fc],
                                                  in_=woutbf_r[:, :, fc * 128:(fc + 1) * 128]),
                                      f"w{fc}", 16)
                            wold[fc] = P.cnt[f"w{fc}"]
                        mulcnt = P.cnt["dve"]
                        if debug and b == 1:
                            @blk.sync
                            def _(s):
                                P.w(s, "dve")
                                P.inc(s.dma_start(out=d_ton,
                                                  in_=onn[:].rearrange("p c n -> p (c n)")),
                                      "st", 16)
                                P.inc(s.dma_start(out=d_tor,
                                                  in_=kob[:].rearrange("p c n -> p (c n)")),
                                      "st", 16)
                                P.inc(s.dma_start(out=d_tyt,
                                                  in_=vob[:, 0, 0:256]), "st", 16)
                                P.inc(s.dma_start(out=d_tpy,
                                                  in_=vob[:, 0, 256:512]), "st", 16)
                                P.inc(s.dma_start(out=d_tfa,
                                                  in_=scr8[:].rearrange("p c n -> p (c n)")),
                                      "st", 16)
                                P.inc(s.dma_start(out=d_trc[0:1, 0:2048], in_=bass.AP(ro_c, 0, [[16, 128], [1, 16]]).rearrange("p c -> (p c)")[None, :] if False else rec0[:]), "st", 16)
                                P.inc(s.dma_start(out=d_trc[1:2, :], in_=rec1[:]), "st", 16)
                                P.inc(s.dma_start(out=d_twt[:, 0:16], in_=ro_c[:]), "st", 16)
                                P.inc(s.dma_start(out=d_twt[:, 16:32], in_=c_cb[:]), "st", 16)
                                P.inc(s.dma_start(out=d_twt[:, 32:48], in_=co_c[:]), "st", 16)
                                P.inc(s.dma_start(out=d_twt[:, 48:64], in_=r_cb[:]), "st", 16)
                                s.wait_ge(P.sem["st"], P.cnt["st"])
                        @blk.sync
                        def _(s, b=b, mulcnt=mulcnt):
                            P.wv(s, "dve", mulcnt)
                            P.inc(s.dma_start(out=scr8[:],
                                              in_=xo_r[:, :, b * 256:(b + 1) * 256]), "rs", 16)
                        scr8ld = P.cnt["rs"]
                        with nc.psum_tensor(f"psB_y0_{b}", [128, 256], F32) as py0, \
                             nc.psum_tensor(f"psB_y1_{b}", [128, 256], F32) as py1:
                            pys = (py0, py1)
                            pyfree = [mulcnt, mulcnt]
                            for fc in range(8):
                                slot4 = fc % 4
                                py = pys[fc % 2]
                                @blk.tensor
                                def _(t, fc=fc, py=py, slot4=slot4, actB6=actB6,
                                      lw=wold[slot4], dw=max(mulcnt, pyfree[fc % 2])):
                                    P.wv(t, f"w{slot4}", lw)
                                    P.wv(t, "dve", dw)
                                    if fc == 0:
                                        P.wv(t, "act", actB6)
                                    for ec in range(NCH):
                                        i = t.matmul(py[:], wt[:, slot4, ec, :], onn[:, ec, :],
                                                     start=(ec == 0), stop=(ec == NCH - 1))
                                    P.inc(i, "pe")
                                ych = P.cnt["pe"]
                                @blk.vector
                                def _(v, fc=fc, py=py, ych=ych, sl=scr8ld, sw=ystor[0]):
                                    P.wv(v, "pe", ych)
                                    P.wv(v, "rs", sl)
                                    P.wv(v, "yp", sw)
                                    v.tensor_add(yt8[:, fc % 2, :], py[:], scr8[:, fc, :])
                                pyfree[fc % 2] = P.cnt["dve"]
                                wtuse[slot4] = P.cnt["pe"]
                                if fc < 4:
                                    @blk.sync
                                    def _(s, fc=fc, slot4=slot4, pw=wtuse[slot4]):
                                        P.wv(s, "pe", pw)
                                        P.inc(s.dma_start(
                                            out=wt[:, slot4],
                                            in_=woutbf_r[:, :, (fc + 4) * 128:(fc + 5) * 128]),
                                            f"w{slot4}", 16)
                                    wold[slot4] = P.cnt[f"w{slot4}"]
                                if fc % 2 == 1:
                                    if debug and b == 7 and fc == 1:
                                        @blk.sync
                                        def _(s, dw=max(pyfree[0], pyfree[1])):
                                            P.wv(s, "dve", dw)
                                            P.inc(s.dma_start(
                                                out=d_tb7[:, 4096:4608],
                                                in_=yt8[:].rearrange("p a n -> p (a n)")),
                                                "st", 16)
                                            P.inc(s.dma_start(
                                                out=d_tb7[:, 2048:4096],
                                                in_=scr8[:].rearrange("p c n -> p (c n)")),
                                                "st", 16)
                                        @blk.sync
                                        def _(s):
                                            P.inc(s.dma_start(
                                                out=d_tz[:, 0:2048],
                                                in_=onn[:].rearrange("p c n -> p (c n)")),
                                                "st", 16)
                                    @blk.sync
                                    def _(s, fc=fc, b=b, dw=max(pyfree[0], pyfree[1])):
                                        P.wv(s, "dve", dw)
                                        P.inc(s.dma_start(
                                            out=d_ypark.ap().rearrange("(c p) n -> p c n", p=128)[
                                                :, fc - 1:fc + 1, b * 256:(b + 1) * 256],
                                            in_=yt8[:]), "yp", 16)
                                    ystor[0] = P.cnt["yp"]
                            if b + 1 < B:
                                for fi in range(4):
                                    @blk.sync
                                    def _(s, fi=fi, pw=wtuse[fi]):
                                        P.wv(s, "pe", pw)
                                        P.inc(s.dma_start(
                                            out=wt[:, fi],
                                            in_=wqkvbf_r[:, :, 1024 + fi * 128:
                                                         1024 + (fi + 1) * 128]), f"w{fi}", 16)
                                    wtld[fi] = P.cnt[f"w{fi}"]
                                for ft in range(2):
                                    @blk.sync
                                    def _(s, ft=ft, pw=wvuse[ft]):
                                        P.wv(s, "pe", pw)
                                        P.inc(s.dma_start(
                                            out=wvres[:, ft],
                                            in_=wqkvbf_r[:, :, 2048 + ft * 256:
                                                         2048 + (ft + 1) * 256]), f"v{ft}", 16)
                                    wvld[ft] = P.cnt[f"v{ft}"]
                        if b + 1 < B:
                            emit_B1(b + 1, xbgp_l[0])
                    w8 = FFN_ORDER[b]
                    for q4 in range(4):
                        @blk.gpsimd
                        def _(g, w8=w8, q4=q4, sw=stage_st[0]):
                            P.wv(g, "sg", sw)
                            P.inc(g.dma_start(out=wstage_b[:],
                                              in_=w_r[w8][:, :, q4 * 256:(q4 + 1) * 256]),
                                  "c1", 16)
                        stage_gp = P.cnt["c1"]
                        @blk.sync
                        def _(s, b=b, q4=q4, gc=stage_gp):
                            P.wv(s, "c1", gc)
                            P.inc(s.dma_start(
                                out=wffnbf_r[:, :, b * 1024 + q4 * 256:
                                             b * 1024 + (q4 + 1) * 256],
                                in_=wstage_b[:]), "sg", 16)
                        stage_st[0] = P.cnt["sg"]
                    if debug and b == 0:
                        @blk.sync
                        def _(s):
                            P.inc(s.dma_start(out=d_tkb, in_=kb[:].rearrange("p c n -> p (c n)")),
                                  "st", 16)
                            P.inc(s.dma_start(out=d_tnb, in_=negb[:]), "st", 16)
                            s.wait_ge(P.sem["st"], P.cnt["st"])
        ypark_r = d_ypark.ap().rearrange("(c p) n -> p c n", p=128)

        with ExitStack() as cctx:
            csb = lambda n, sh, dt: cctx.enter_context(nc.sbuf_tensor(n, sh, dt))
            z = csb("z", [128, NCH, OWN], BF16)
            cuo = csb("cuo", [128, NCH, 256], F32)
            wt_c = csb("wt_c", [128, 2, NCH, 128], BF16)
            ysl = csb("ysl", [128, 2, NCH, 256], F32)
            tmp_c = csb("tmp_c", [128, 2, 256], F32)
            ybf_v = [z[:, :, 0:256], z[:, :, 256:512]]
            ysq_v = [z[:, :, 512:768], z[:, :, 768:1024]]

            yslld = [0, 0]
            ysluse = [0, 0]
            ystuse = [0, 0]
            ypdone = P.cnt["yp"]
            with nc.psum_tensor("psC_su", [1, 2048], F32) as psu, \
                 nc.psum_tensor("psC_sq", [1, 2048], F32) as psq:
                for sl in range(8):
                    k = sl % 2
                    @blk.sync
                    def _(s, sl=sl, k=k, du=ysluse[k], sw=ypdone):
                        P.wv(s, "dve", du)
                        P.wv(s, "yp", sw)
                        P.inc(s.dma_start(out=ysl[:, k],
                                          in_=ypark_r[:, :, sl * 256:(sl + 1) * 256]), f"y{k}", 16)
                    yslld[k] = P.cnt[f"y{k}"]
                    @blk.vector
                    def _(v, k=k, lw=yslld[k], pw=ystuse[k]):
                        P.wv(v, f"y{k}", lw)
                        P.wv(v, "pe", pw)
                        v.tensor_copy(ybf_v[k], ysl[:, k])
                        v.tensor_mul(ysq_v[k], ysl[:, k], ysl[:, k])
                    ysluse[k] = P.cnt["dve"]
                    @blk.tensor
                    def _(t, sl=sl, k=k, dw=ysluse[k]):
                        P.wv(t, "dve", dw)
                        for c in range(NCH):
                            t.matmul(psu[:, sl * 256:(sl + 1) * 256], invd_bf[:],
                                     ybf_v[k][:, c, :], start=(c == 0), stop=(c == NCH - 1))
                            i = t.matmul(psq[:, sl * 256:(sl + 1) * 256], invd_bf[:],
                                         ysq_v[k][:, c, :], start=(c == 0), stop=(c == NCH - 1))
                        P.inc(i, "pe")
                    ystuse[k] = P.cnt["pe"]
                @blk.vector
                def _(v):
                    P.w(v, "pe")
                    v.tensor_copy(rw1[:], psu[:])
                    v.tensor_copy(rw2[:], psq[:])
            @blk.vector
            def _(v):
                for sl in range(8):
                    v.tensor_mul(tmp_c[0:1, 0, 0:256], rw1[:, sl * 256:(sl + 1) * 256],
                                 rw1[:, sl * 256:(sl + 1) * 256])
                    v.tensor_sub(rw2[:, sl * 256:(sl + 1) * 256],
                                 rw2[:, sl * 256:(sl + 1) * 256], tmp_c[0:1, 0, 0:256])
            @blk.scalar
            def _(sc):
                P.w(sc, "dve")
                sc.activation(rw2[:], rw2[:], AF.Sqrt, bias=epsc[:])
            @blk.vector
            def _(v):
                P.w(v, "act")
                v.reciprocal(rw2[:], rw2[:])
            @blk.gpsimd
            def _(g):
                P.w(g, "act", "dve", "gp")
                P.inc(g.dma_start(out=d_rowb.ap(), in_=rw1[:]), "gp", 16)
                g.wait_ge(P.sem["gp"], P.cnt["gp"])
                P.inc(g.dma_start(out=d_recb.ap()[0:1, :], in_=rw2[:]), "gp", 16)
            with nc.sbuf_tensor("mu2bc", [128, OWN], F32) as mu2bc, \
                 nc.sbuf_tensor("r2bc", [128, OWN], F32) as r2bc:
                @blk.gpsimd
                def _(g):
                    P.w(g, "gp")
                    P.inc(g.dma_start(out=mu2bc[:],
                                      in_=d_rowb.ap().partition_broadcast(128)[:, 0, :]),
                          "gp", 16)
                    P.inc(g.dma_start(out=r2bc[:],
                                      in_=bass.AP(d_recb, 0, [[0, 128], [1, OWN]])), "gp", 16)
                zbgp = P.cnt["gp"]
                for sl in range(8):
                    k = sl % 2
                    @blk.sync
                    def _(s, sl=sl, k=k, du=ysluse[k], sw=ypdone):
                        P.wv(s, "dve", du)
                        P.wv(s, "yp", sw)
                        P.inc(s.dma_start(out=ysl[:, k],
                                          in_=ypark_r[:, :, sl * 256:(sl + 1) * 256]), f"y{k}", 16)
                    yslld[k] = P.cnt[f"y{k}"]
                    @blk.vector
                    def _(v, sl=sl, k=k, lw=yslld[k], gw=zbgp):
                        P.wv(v, "gp", gw)
                        P.wv(v, f"y{k}", lw)
                        mub = bass.AP(mu2bc, sl * 256, [list(mu2bc.ap().ap[0]), [0, NCH], [1, 256]])
                        rbb = bass.AP(r2bc, sl * 256, [list(r2bc.ap().ap[0]), [0, NCH], [1, 256]])
                        v.tensor_sub(ysl[:, k], ysl[:, k], mub)
                        v.tensor_mul(z[:, :, sl * 256:(sl + 1) * 256], ysl[:, k], rbb)
                    ysluse[k] = P.cnt["dve"]
            cst = csb("cst", [128, NCH, 256], F32)
            cbf = csb("cbf", [128, NCH, 256], BF16)
            gates = {"f": csb("g_f", [128, NCH, 256], F32),
                     "i": ysl[:, 0], "c": ysl[:, 1]}
            rctx = ExitStack()
            wres = {w: rctx.enter_context(nc.sbuf_tensor("w_" + w, [128, NCH, D], BF16))
                    for w in ("wf", "wi", "wc", "uf", "ui", "uc")}

            @blk.sync
            def _(s):
                P.w(s, "dve")
                for wi_, w8 in enumerate(("wf", "wi", "wc", "uf", "ui", "uc")):
                    P.inc(s.dma_start(out=wres[w8][:],
                                      in_=wffnbf_r[:, :, wi_ * 1024:(wi_ + 1) * 1024]),
                          "c0", 16)
            for c in range(NCH):
                @blk.vector
                def _(v, c=c):
                    P.w(v, "gp")
                    src = bass.AP(bv["incv"], c, [list(bv["incv"].ap().ap[0]), [0, 256]])
                    P.inc(v.tensor_copy(cst[:, c, :], src), "dve")
            @blk.vector
            def _(v):
                P.w(v, "dve", "pool")
                P.inc(v.tensor_copy(cbf[:], cst[:]), "dve")
            gnames = (("f", "wf", "uf", "bfv", AF.Sigmoid),
                      ("i", "wi", "ui", "biv", AF.Sigmoid),
                      ("c", "wc", "uc", "bcv", AF.Tanh))
            with nc.psum_tensor("psC_g0", [128, 256], F32) as pg0, \
                 nc.psum_tensor("psC_g1", [128, 256], F32) as pg1, \
                 nc.psum_tensor("psC_g2", [128, 256], F32) as pg2, \
                 nc.psum_tensor("psC_g3", [128, 256], F32) as pg3:
                pgs = (pg0, pg1, pg2, pg3)
                pguse = [0, 0, 0, 0]
                gi = 0
                for t_ in range(B):
                    for (gn, wn, un, bn, af) in gnames:
                        act_g = P.cnt["act"]
                        chc = []
                        bis = []
                        for mc in range(NCH):
                            bi = gi % 4
                            gi += 1
                            bis.append(bi)
                            pg = pgs[bi]
                            aw = pguse[bi] if mc < 4 else act_g + mc - 3
                            @blk.tensor
                            def _(t, t_=t_, wn=wn, un=un, mc=mc, pg=pg, aw=aw,
                                  first=(mc == 0 and gn == "f")):
                                P.wv(t, "act", aw)
                                if first:
                                    P.w(t, "dve", "pool", "c0")
                                for ec in range(NCH):
                                    t.matmul(pg[:], wres[wn][:, ec, mc * 128:(mc + 1) * 128],
                                             z[:, ec, t_ * 256:(t_ + 1) * 256],
                                             start=(ec == 0), stop=False)
                                for ec in range(NCH):
                                    i = t.matmul(pg[:], wres[un][:, ec, mc * 128:(mc + 1) * 128],
                                                 cbf[:, ec, :],
                                                 start=False, stop=(ec == NCH - 1))
                                P.inc(i, "pe")
                            chc.append(P.cnt["pe"])
                        @blk.scalar
                        def _(scg, gn=gn, bn=bn, af=af, chc=tuple(chc), bis=tuple(bis)):
                            for mc in range(NCH):
                                P.wv(scg, "pe", chc[mc])
                                scg.activation(gates[gn][:, mc, :], pgs[bis[mc]][:], af,
                                               bias=bv[bn][:, mc:mc + 1])
                        assert P.cnt["act"] == act_g + NCH
                        for i_, bi in enumerate(bis):
                            pguse[bi] = act_g + i_ + 1
                    @blk.vector
                    def _(v, t_=t_):
                        P.w(v, "act")
                        v.tensor_mul(cst[:], cst[:], gates["f"][:])
                        v.tensor_mul(gates["f"][:], gates["i"][:], gates["c"][:])
                        v.tensor_add(cst[:], cst[:], gates["f"][:])
                        v.tensor_copy(cbf[:], cst[:])
            rctx.close()
            with ExitStack() as octx:
                wo_res = octx.enter_context(nc.sbuf_tensor("wo_res", [128, NCH, D], BF16))
                ystage = octx.enter_context(nc.sbuf_tensor("ystage", [128, NCH, 256], F32))
                yo_r = d_yo.rearrange("(c p) n -> p c n", p=128)
                gatedone = P.cnt["pe"]
                @blk.sync
                def _(s, pw=gatedone):
                    P.wv(s, "pe", pw)
                    P.inc(s.dma_start(out=wo_res[:],
                                      in_=wffnbf_r[:, :, 6 * 1024:7 * 1024]), "c0", 16)
                wo_ld = P.cnt["c0"]
                wcuse = [0, 0]
                wtcld = [0, 0]
                for mc in range(NCH):
                    @blk.sync
                    def _(s, mc=mc, pw=wcuse[mc % 2]):
                        P.wv(s, "pe", pw)
                        if mc == 0:
                            P.w(s, "dve")
                        sname = "c1" if mc % 2 == 0 else "qb"
                        P.inc(s.dma_start(out=wt_c[:, mc % 2],
                                          in_=wffnbf_r[:, :, 7 * 1024 + mc * 128:
                                                       7 * 1024 + (mc + 1) * 128]), sname, 16)
                    wtcld[mc % 2] = P.cnt["c1" if mc % 2 == 0 else "qb"]
                    with nc.psum_tensor(f"psC_uo_{mc}", [128, 256], F32) as pu:
                        @blk.tensor
                        def _(t, mc=mc, pu=pu, lw=wtcld[mc % 2]):
                            P.wv(t, "c1" if mc % 2 == 0 else "qb", lw)
                            P.w(t, "dve", "pool")
                            for ec in range(NCH):
                                i = t.matmul(pu[:], wt_c[:, mc % 2, ec, :], cbf[:, ec, :],
                                             start=(ec == 0), stop=(ec == NCH - 1))
                            P.inc(i, "pe")
                        wcuse[mc % 2] = P.cnt["pe"]
                        @blk.vector
                        def _(v, mc=mc, pu=pu, pw=wcuse[mc % 2]):
                            P.wv(v, "pe", pw)
                            v.tensor_copy(cuo[:, mc, :], pu[:])
                potmp = [P.cnt["dve"], P.cnt["dve"]]
                tnh = [P.cnt["act"], P.cnt["act"]]
                tmpw = [P.cnt["act"], P.cnt["act"]]
                ystore = [0]
                with nc.psum_tensor("psC_o0", [128, 256], F32) as po0, \
                     nc.psum_tensor("psC_o1", [128, 256], F32) as po1:
                    pos = (po0, po1)
                    for bt in range(B):
                        k = bt % 2
                        @blk.sync
                        def _(s, bt=bt, k=k, du=ysluse[k], sw=ypdone):
                            P.wv(s, "dve", du)
                            P.wv(s, "yp", sw)
                            if bt < 2:
                                P.w(s, "dve", "pool")
                            P.inc(s.dma_start(out=ysl[:, k],
                                              in_=ypark_r[:, :, bt * 256:(bt + 1) * 256]),
                                  f"y{k}", 16)
                        yslld[k] = P.cnt[f"y{k}"]
                        actb = P.cnt["act"]
                        dve_base = P.cnt["dve"]
                        @blk.scalar
                        def _(scg, actb=actb, dve_base=dve_base):
                            for mc in range(NCH):
                                P.wv(scg, "dve", dve_base + 2 * mc + 1)
                                scg.activation(tmp_c[:, mc % 2, :], tmp_c[:, mc % 2, :],
                                               AF.Tanh, bias=bv["bov"][:, mc:mc + 1])
                        assert P.cnt["act"] == actb + NCH
                        for mc in range(NCH):
                            m2 = mc % 2
                            po = pos[m2]
                            @blk.tensor
                            def _(t, mc=mc, bt=bt, po=po, lw=wo_ld, dw=potmp[m2]):
                                P.wv(t, "c0", lw)
                                P.wv(t, "dve", dw)
                                for ec in range(NCH):
                                    i = t.matmul(po[:], wo_res[:, ec, mc * 128:(mc + 1) * 128],
                                                 z[:, ec, bt * 256:(bt + 1) * 256],
                                                 start=(ec == 0), stop=(ec == NCH - 1))
                                P.inc(i, "pe")
                            och = P.cnt["pe"]
                            @blk.vector
                            def _(v, mc=mc, m2=m2, po=po, och=och,
                                  aw=actb + max(0, mc - 1)):
                                P.wv(v, "pe", och)
                                P.wv(v, "act", aw)
                                P.inc(v.tensor_add(tmp_c[:, m2, :], po[:], cuo[:, mc, :]),
                                      "dve")
                            potmp[m2] = P.cnt["dve"]
                            assert potmp[m2] == dve_base + 2 * mc + 1
                            @blk.vector
                            def _(v, mc=mc, m2=m2, k=k, aw=actb + mc + 1, lw=yslld[k],
                                  sw=ystore[0]):
                                P.wv(v, "act", aw)
                                P.wv(v, f"y{k}", lw)
                                P.wv(v, "rs", sw)
                                P.inc(v.tensor_add(ystage[:, mc, :], tmp_c[:, m2, :],
                                                   ysl[:, k, mc, :]), "dve")
                        ysluse[k] = P.cnt["dve"]
                        @blk.sync
                        def _(s, bt=bt, dw=P.cnt["dve"]):
                            P.wv(s, "dve", dw)
                            P.inc(s.dma_start(out=yo_r[:, :, bt * 256:(bt + 1) * 256],
                                              in_=ystage[:, :, :]), "rs", 16)
                        ystore[0] = P.cnt["rs"]
        @blk.sync
        def _(s):
            P.wall(s)
    return nc


_NC_CACHE = {}


def kernel(L0_attn_query, L0_attn_key, L0_attn_value, L0_attn_in_proj_weight,
           L0_attn_out_proj_weight, L0_ffn_wf, L0_ffn_uf, L0_ffn_bf, L0_ffn_wi,
           L0_ffn_ui, L0_ffn_bi, L0_ffn_wc, L0_ffn_uc, L0_ffn_bc, L0_ffn_wo,
           L0_ffn_uo, L0_ffn_bo, L0_ffn_in_c):
    x = np.asarray(L0_attn_query, np.float32)
    xf = np.ascontiguousarray(x.transpose(2, 0, 1).reshape(D, B * T))
    wqkv = np.ascontiguousarray(np.asarray(L0_attn_in_proj_weight, np.float32).T)
    wout = np.ascontiguousarray(np.asarray(L0_attn_out_proj_weight, np.float32).T)
    wmats = {k: np.ascontiguousarray(np.asarray(v, np.float32).T) for k, v in
             dict(wf=L0_ffn_wf, wi=L0_ffn_wi, wc=L0_ffn_wc, uf=L0_ffn_uf,
                  ui=L0_ffn_ui, uc=L0_ffn_uc, wo=L0_ffn_wo, uo=L0_ffn_uo).items()}
    bvecs = {k: np.ascontiguousarray(np.asarray(v, np.float32).reshape(NCH, 128).T)
             for k, v in dict(bfv=L0_ffn_bf, biv=L0_ffn_bi, bcv=L0_ffn_bc,
                              bov=L0_ffn_bo, incv=L0_ffn_in_c).items()}
    jj = np.arange(128)
    trimask = np.where(jj[:, None] > jj[None, :], np.float32(NEG), np.float32(0.0))
    trimask = np.ascontiguousarray(trimask.astype(np.float32))

    in_maps = []
    for core in range(8):
        blks = (core, 8 + core)
        xo = np.empty((D, OWN), np.float32)
        for b in range(B):
            for s_, blkid in enumerate(blks):
                xo[:, b * 256 + s_ * 128: b * 256 + (s_ + 1) * 128] = \
                    x[b, blkid * 128:(blkid + 1) * 128, :].T
        deny = np.zeros((128, 32), np.float32)
        for s_, blkid in enumerate(blks):
            for k in range(16):
                if k >= blkid:
                    deny[:, s_ * 16 + k] = NEG
        m = dict(xf=xf, xo=np.ascontiguousarray(xo), wqkv=wqkv, wout=wout,
                 deny=deny, trimask=trimask, iden=np.eye(128, dtype=np.float32))
        m.update(wmats)
        m.update(bvecs)
        in_maps.append(m)

    dbg = bool(_NC_CACHE.get("debug"))
    key = ("nc", dbg)
    if key not in _NC_CACHE:
        _NC_CACHE[key] = build_nc(debug=dbg)
    nc = _NC_CACHE[key]
    res = run_bass_kernel_spmd(nc, in_maps, core_ids=list(range(8)))

    if dbg:
        _NC_CACHE["results"] = res.results
    out = np.empty((B, T, D), np.float32)
    for core in range(8):
        yo = res.results[core]["yo"]
        for b in range(B):
            for s_, blkid in enumerate((core, 8 + core)):
                out[b, blkid * 128:(blkid + 1) * 128, :] = \
                    yo[:, b * 256 + s_ * 128: b * 256 + (s_ + 1) * 128].T
    return out


# version 23
# speedup vs baseline: 1.0154x; 1.0154x over previous
import numpy as np
from contextlib import ExitStack
import concourse.bass as bass
import concourse.mybir as mybir
from concourse.bass_utils import run_bass_kernel_spmd

F32 = mybir.dt.float32
BF16 = mybir.dt.bfloat16
AF = mybir.ActivationFunctionType
ALU = mybir.AluOpType
AX = mybir.AxisListType

D, NCH, B, T, H = 1024, 8, 8, 2048, 16
OWN = 2048
NEG = -1.0e30
EPS = 1e-5


class P_:
    def __init__(self, nc, ctx):
        self.sem = {}
        self.cnt = {}
        for n in ("ld", "gp", "pe", "act", "dve", "pool", "st",
                  "w0", "w1", "w2", "w3", "v0", "v1", "qb", "rs",
                  "y0", "y1", "c0", "c1", "yp", "sg", "q0", "q1", "xb",
                  "x0", "x1", "x2", "x3"):
            self.sem[n] = ctx.enter_context(nc.semaphore(n))
            self.cnt[n] = 0
        self.tagged = set()
        self._keep = []

    def inc(self, instr, name, amt=None):
        if id(instr) in self.tagged:
            return
        amt = amt if amt is not None else 1
        instr.then_inc(self.sem[name], amt)
        self.cnt[name] += amt

    def wv(self, eng, name, val):
        if val > 0:
            eng.wait_ge(self.sem[name], val)

    def w(self, eng, *names):
        for n in names:
            if self.cnt[n] > 0:
                eng.wait_ge(self.sem[n], self.cnt[n])

    def wall(self, eng):
        self.w(eng, *self.sem.keys())


_PASSTHRU = ("wait_ge", "sem_inc", "nop", "dma_start", "dma_start_transpose",
             "indirect_dma_start", "trigger_dma", "collective_compute")


class SeqProxy:

    def __init__(self, eng, P, nm):
        self._eng, self._P, self._nm = eng, P, nm

    def __getattr__(self, name):
        f = getattr(self._eng, name)
        if name.startswith("_") or name in _PASSTHRU or not callable(f):
            return f

        def wrap(*args, **kw):
            P, nm = self._P, self._nm
            if P.cnt[nm] > 0:
                self._eng.wait_ge(P.sem[nm], P.cnt[nm])
            instr = f(*args, **kw)
            instr.then_inc(P.sem[nm], 1)
            P.cnt[nm] += 1
            P.tagged.add(id(instr))
            P._keep.append(instr)
            return instr
        return wrap


class BlkWrap:
    def __init__(self, blk, P):
        self._blk, self._P = blk, P

    @property
    def sync(self):
        return self._blk.sync

    @property
    def tensor(self):
        return self._blk.tensor

    @property
    def vector(self):
        return lambda fn: self._blk.vector(lambda e: fn(SeqProxy(e, self._P, "dve")))

    @property
    def scalar(self):
        return lambda fn: self._blk.scalar(lambda e: fn(SeqProxy(e, self._P, "act")))

    @property
    def gpsimd(self):
        return lambda fn: self._blk.gpsimd(lambda e: fn(SeqProxy(e, self._P, "pool")))


def build_nc(debug=False):
    nc = bass.Bass("TRN2", target_bir_lowering=False)

    d_xf = nc.dram_tensor("xf", [D, B * T], F32, kind="ExternalInput").ap()
    d_xo = nc.dram_tensor("xo", [D, OWN], F32, kind="ExternalInput").ap()
    d_wqkv = nc.dram_tensor("wqkv", [D, 3 * D], F32, kind="ExternalInput").ap()
    d_wout = nc.dram_tensor("wout", [D, D], F32, kind="ExternalInput").ap()
    d_w = {w: nc.dram_tensor(w, [D, D], F32, kind="ExternalInput").ap()
           for w in ("wf", "wi", "wc", "uf", "ui", "uc", "wo", "uo")}
    d_bv = {v: nc.dram_tensor(v, [128, NCH], F32, kind="ExternalInput").ap()
            for v in ("bfv", "biv", "bcv", "bov", "incv")}
    d_deny = nc.dram_tensor("deny", [128, 32], F32, kind="ExternalInput").ap()
    d_tri = nc.dram_tensor("trimask", [128, 128], F32, kind="ExternalInput").ap()
    d_iden = nc.dram_tensor("iden", [128, 128], F32, kind="ExternalInput").ap()
    d_yo = nc.dram_tensor("yo", [D, OWN], F32, kind="ExternalOutput").ap()

    dkind = "ExternalOutput" if debug else "Internal"
    d_wqkv_bf = nc.dram_tensor("wqkv_bf", [D, 3 * D], BF16, kind=dkind)
    d_wout_bf = nc.dram_tensor("wout_bf", [D, D], BF16, kind=dkind)
    d_wffn_bf = nc.dram_tensor("wffn_bf", [D, 8 * D], BF16, kind=dkind)
    d_qpark = nc.dram_tensor("q_park", [D, OWN], BF16, kind=dkind)
    d_ypark = nc.dram_tensor("y_park", [D, OWN], F32, kind=dkind)
    if debug:
        d_txh = nc.dram_tensor("t_xhat", [128, NCH * OWN], BF16, kind="ExternalOutput").ap()
        d_tkb = nc.dram_tensor("t_kb", [128, NCH * T], BF16, kind="ExternalOutput").ap()
        d_ton = nc.dram_tensor("t_onn", [128, NCH * 256], BF16, kind="ExternalOutput").ap()
        d_tor = nc.dram_tensor("t_onraw", [128, NCH * 256], BF16, kind="ExternalOutput").ap()
        d_trc = nc.dram_tensor("t_rec", [2, 2048], F32, kind="ExternalOutput").ap()
        d_tfa = nc.dram_tensor("t_fac", [128, NCH * 256], F32, kind="ExternalOutput").ap()
        d_tyt = nc.dram_tensor("t_yt", [128, 256], BF16, kind="ExternalOutput").ap()
        d_txr = nc.dram_tensor("t_xr", [128, 256], F32, kind="ExternalOutput").ap()
        d_tpy = nc.dram_tensor("t_py", [128, 256], BF16, kind="ExternalOutput").ap()
        d_twt = nc.dram_tensor("t_wt", [128, 128], F32, kind="ExternalOutput").ap()
        d_tnb = nc.dram_tensor("t_nb", [128, 1], F32, kind="ExternalOutput").ap()
        d_tz = nc.dram_tensor("t_z", [128, NCH * OWN], BF16, kind="ExternalOutput").ap()
        d_tb7 = nc.dram_tensor("t_b7", [128, 4 * 2048 + 512], F32, kind="ExternalOutput").ap()
    d_rowb = nc.dram_tensor("row_bounce", [1, T], F32)
    d_recb = nc.dram_tensor("rec_bounce", [2, 2048], F32)

    xf_r = d_xf.rearrange("(c p) n -> p c n", p=128)
    xo_r = d_xo.rearrange("(c p) n -> p c n", p=128)
    wqkv_r = d_wqkv.rearrange("(c p) f -> p c f", p=128)
    wout_r = d_wout.rearrange("(c p) f -> p c f", p=128)
    w_r = {k: v.rearrange("(c p) f -> p c f", p=128) for k, v in d_w.items()}
    qpark_r = d_qpark.ap().rearrange("(c p) n -> p c n", p=128)
    wqkvbf_r = d_wqkv_bf.ap().rearrange("(c p) f -> p c f", p=128)
    woutbf_r = d_wout_bf.ap().rearrange("(c p) f -> p c f", p=128)
    wffnbf_r = d_wffn_bf.ap().rearrange("(c p) f -> p c f", p=128)
    FFN_ORDER = ("wf", "wi", "wc", "uf", "ui", "uc", "wo", "uo")

    with ExitStack() as ctx:
        sb = lambda n, sh, dt: ctx.enter_context(nc.sbuf_tensor(n, sh, dt))
        tri = sb("tri", [128, 128], F32)
        iden = sb("iden_s", [128, 128], F32)
        deny = sb("deny_s", [128, 32], F32)
        ones_bf = sb("ones_bf", [128, 1], BF16)
        invd_bf = sb("invd_bf", [128, 1], BF16)
        ones_f1 = sb("ones_f1", [1, 128], F32)
        idy = sb("idy", [1, 1], F32)
        bv = {v: sb(v + "_s", [128, NCH], F32) for v in d_bv}
        ro_c = sb("ro_c", [128, 16], F32)
        lo_c = sb("lo_c", [128, 16], F32)
        co_c = sb("co_c", [128, 16], F32)
        qmax2 = sb("qmax2", [1, 1], F32)
        epsc = sb("epsc", [1, 1], F32)
        rw1 = sb("rw1", [1, T], F32)
        rw2 = sb("rw2", [1, T], F32)

        P = P_(nc, ctx)
        blk = BlkWrap(ctx.enter_context(nc.Block()), P)

        @blk.sync
        def _(s):
            P.inc(s.dma_start(out=tri[:], in_=d_tri), "ld", 16)
            P.inc(s.dma_start(out=iden[:], in_=d_iden), "ld", 16)
            P.inc(s.dma_start(out=deny[:], in_=d_deny), "ld", 16)
            for v in bv:
                P.inc(s.dma_start(out=bv[v][:], in_=d_bv[v]), "ld", 16)

        @blk.vector
        def _(v):
            v.memset(ones_bf[:], 1.0)
            v.memset(invd_bf[:], 1.0 / D)
            v.memset(ones_f1[:], 1.0)
            v.memset(idy[:], 1.0)
            v.memset(epsc[:], EPS)
            P.inc(v.memset(qmax2[:], 0.0), "dve")

        with ExitStack() as abctx:
            xa_bf = abctx.enter_context(nc.sbuf_tensor("xa_bf", [128, NCH, OWN], BF16))
            nmu_own_bf = abctx.enter_context(nc.sbuf_tensor("nmu_own_bf", [1, OWN], BF16))

            with ExitStack() as actx:
                asb = lambda n, sh, dt: actx.enter_context(nc.sbuf_tensor(n, sh, dt))
                x2sl = asb("x2sl_a", [128, NCH, 512], BF16)
                wq = asb("wq_a", [128, NCH, 1024], BF16)
                nwq = asb("nwq_a", [1, 1024], BF16)
                qev = asb("qev_a", [128, 2, 512], BF16)
                q2 = asb("q2_a", [128, 2, 512], BF16)
                rbc = asb("rbc_a", [128, 512], F32)
                qscr = asb("qscr_a", [1, 4], F32)

                wstage = asb("wstage_a", [128, NCH, 1024], BF16)

                xachk = []
                @blk.gpsimd
                def _(g):
                    for sl in range(4):
                        P.inc(g.dma_start(out=xa_bf[:, :, sl * 512:(sl + 1) * 512],
                                          in_=xo_r[:, :, sl * 512:(sl + 1) * 512]),
                              f"x{sl}", 16)
                        xachk.append(P.cnt[f"x{sl}"])
                    P.inc(g.dma_start(out=wq[:], in_=wqkv_r[:, :, 0:1024]), "gp", 16)
                for seg in range(4):
                    srcv = (wqkv_r[:, :, seg * 1024:(seg + 1) * 1024] if seg < 3
                            else wout_r[:, :, 0:1024])
                    dstv = (d_wqkv_bf.ap().rearrange("(c p) f -> p c f", p=128)[
                                :, :, seg * 1024:(seg + 1) * 1024] if seg < 3
                            else woutbf_r[:, :, 0:1024])
                    @blk.gpsimd
                    def _(g, srcv=srcv, seg=seg):
                        P.w(g, "st")
                        P.inc(g.dma_start(out=wstage[:], in_=srcv), "sg", 16)
                    sgc = P.cnt["sg"]
                    @blk.sync
                    def _(s, dstv=dstv, sgc=sgc):
                        P.wv(s, "sg", sgc)
                        P.inc(s.dma_start(out=dstv, in_=wstage[:]), "st", 16)

                with nc.psum_tensor("psA_su", [1, 2048], F32) as psu, \
                     nc.psum_tensor("psA_sq", [1, 2048], F32) as psq:
                    for sl in range(4):
                        @blk.vector
                        def _(v, sl=sl, lw=None):
                            P.wv(v, f"x{sl}", xachk[sl])
                            P.w(v, "pe")
                            v.tensor_mul(x2sl[:], xa_bf[:, :, sl * 512:(sl + 1) * 512],
                                         xa_bf[:, :, sl * 512:(sl + 1) * 512])
                        @blk.tensor
                        def _(t, sl=sl):
                            P.w(t, "dve")
                            for c in range(NCH):
                                t.matmul(psu[:, sl * 512:(sl + 1) * 512], invd_bf[:],
                                         xa_bf[:, c, sl * 512:(sl + 1) * 512],
                                         start=(c == 0), stop=(c == NCH - 1))
                                i = t.matmul(psq[:, sl * 512:(sl + 1) * 512], invd_bf[:],
                                             x2sl[:, c, :], start=(c == 0), stop=(c == NCH - 1))
                            P.inc(i, "pe")
                    @blk.vector
                    def _(v):
                        P.w(v, "pe")
                        v.tensor_copy(rw1[:], psu[:])
                        v.tensor_copy(rw2[:], psq[:])
                        v.tensor_scalar_mul(nmu_own_bf[:], rw1[:], -1.0)
                @blk.vector
                def _(v):
                    for sl in range(4):
                        v.tensor_mul(rbc[0:1, :], rw1[:, sl * 512:(sl + 1) * 512],
                                     rw1[:, sl * 512:(sl + 1) * 512])
                        v.tensor_sub(rw2[:, sl * 512:(sl + 1) * 512],
                                     rw2[:, sl * 512:(sl + 1) * 512], rbc[0:1, 0:512])
                @blk.scalar
                def _(sc):
                    P.w(sc, "dve")
                    sc.activation(rw2[:], rw2[:], AF.Sqrt, bias=epsc[:])
                with nc.psum_tensor("psA_tr", [128, 16], F32) as ptr:
                    @blk.tensor
                    def _(t):
                        P.w(t, "act")
                        for jt in range(16):
                            i = t.transpose(ptr[:, jt:jt + 1], rw2[:, jt * 128:(jt + 1) * 128],
                                            idy[:])
                        P.inc(i, "pe")
                    @blk.vector
                    def _(v):
                        P.w(v, "pe")
                        v.reciprocal(ro_c[:], ptr[:])
                        v.tensor_copy(co_c[:], ptr[:])
                        v.reciprocal(rw1[:], rw2[:])
                        v.tensor_scalar_mul(rw1[:], rw1[:], 0.125)
                @blk.scalar
                def _(sc):
                    P.w(sc, "dve")
                    sc.activation(lo_c[:], ro_c[:], AF.Ln)
                @blk.gpsimd
                def _(g):
                    P.w(g, "dve")
                    P.inc(g.dma_start(out=d_rowb.ap(), in_=rw1[:]), "gp", 16)
                with nc.psum_tensor("psA_ws", [1, 128], F32) as pws:
                    for fc in range(8):
                        @blk.tensor
                        def _(t, fc=fc):
                            P.w(t, "gp")
                            if fc:
                                P.w(t, "dve")
                            for ec in range(NCH):
                                i = t.matmul(pws[:], ones_bf[:],
                                             wq[:, ec, fc * 128:(fc + 1) * 128],
                                             start=(ec == 0), stop=(ec == NCH - 1))
                            P.inc(i, "pe")
                        @blk.vector
                        def _(v, fc=fc):
                            P.w(v, "pe")
                            v.tensor_copy(nwq[:, fc * 128:(fc + 1) * 128], pws[:])
                nwqdve = P.cnt["dve"]
                pqevac = [P.cnt["dve"], P.cnt["dve"]]
                qparkst = [0, 0]
                rbcgp = 0
                with nc.psum_tensor("psA_q0", [128, 512], F32) as pq0, \
                     nc.psum_tensor("psA_q1", [128, 512], F32) as pq1, \
                     nc.psum_tensor("psA_n0", [1, 512], F32) as pn0, \
                     nc.psum_tensor("psA_n1", [1, 512], F32) as pn1:
                    pqs = (pq0, pq1)
                    for sl in range(4):
                        @blk.gpsimd
                        def _(g, sl=sl, w0=pqevac[0], w1=pqevac[1]):
                            P.w(g, "gp")
                            P.wv(g, "dve", max(w0, w1))
                            P.inc(g.dma_start(
                                out=rbc[:],
                                in_=d_rowb.ap().partition_broadcast(128)[:, 0, sl * 512:(sl + 1) * 512]),
                                "rs", 16)
                        rbcgp = P.cnt["rs"]
                        for fc in range(8):
                            k = fc % 2
                            pq = pqs[k]
                            @blk.tensor
                            def _(t, fc=fc, sl=sl, pq=pq, dw=max(nwqdve, pqevac[k]),
                                  gw=rbcgp):
                                P.wv(t, "rs", gw)
                                P.wv(t, "dve", dw)
                                for c in range(NCH):
                                    t.matmul(pq[:], wq[:, c, fc * 128:(fc + 1) * 128],
                                             xa_bf[:, c, sl * 512:(sl + 1) * 512],
                                             start=(c == 0), stop=False)
                                i = t.matmul(pq[:], nwq[:, fc * 128:(fc + 1) * 128],
                                             nmu_own_bf[:, sl * 512:(sl + 1) * 512],
                                             start=False, stop=True)
                                P.inc(i, "pe")
                            qch = P.cnt["pe"]
                            @blk.vector
                            def _(v, k=k, pq=pq, qch=qch, sw=qparkst[k]):
                                P.wv(v, "pe", qch)
                                P.wv(v, f"q{k}", sw)
                                v.tensor_mul(qev[:, k, :], pq[:], rbc[:])
                                v.tensor_mul(q2[:, k, :], qev[:, k, :], qev[:, k, :])
                            pqevac[k] = P.cnt["dve"]
                            @blk.sync
                            def _(s, fc=fc, sl=sl, k=k, dw=pqevac[k]):
                                P.wv(s, "dve", dw)
                                P.inc(s.dma_start(
                                    out=d_qpark.ap()[fc * 128:(fc + 1) * 128,
                                                     sl * 512:(sl + 1) * 512],
                                    in_=qev[:, k, :]), f"q{k}", 16)
                            qparkst[k] = P.cnt[f"q{k}"]
                            @blk.tensor
                            def _(t, k=k, dw=pqevac[k]):
                                P.wv(t, "dve", dw)
                                t.matmul(pn0[:], ones_bf[0:64, :], q2[0:64, k, :],
                                         start=True, stop=True)
                                P.inc(t.matmul(pn1[:], ones_bf[64:128, :], q2[64:128, k, :],
                                               start=True, stop=True), "pe")
                            pnch = P.cnt["pe"]
                            @blk.vector
                            def _(v, pnch=pnch):
                                P.wv(v, "pe", pnch)
                                v.tensor_reduce(qscr[:, 0:1], pn0[:], axis=AX.X, op=ALU.max)
                                v.tensor_reduce(qscr[:, 1:2], pn1[:], axis=AX.X, op=ALU.max)
                                v.tensor_max(qscr[:, 0:1], qscr[:, 0:1], qscr[:, 1:2])
                                v.tensor_max(qmax2[:], qmax2[:], qscr[:, 0:1])
            for dec in (blk.sync, blk.gpsimd, blk.tensor, blk.scalar, blk.vector):
                @dec
                def _(e):
                    P.wall(e)
            @blk.scalar
            def _(sc):
                P.w(sc, "dve")
                P.inc(sc.activation(qmax2[:], qmax2[:], AF.Sqrt), "act")
            if debug:
                @blk.sync
                def _(s):
                    P.inc(s.dma_start(out=d_txh, in_=xa_bf[:].rearrange("p c n -> p (c n)")),
                          "st", 16)
                    s.wait_ge(P.sem["st"], P.cnt["st"])

            with ExitStack() as bctx:
                bsb = lambda n, sh, dt: bctx.enter_context(nc.sbuf_tensor(n, sh, dt))
                xb_bf = bsb("xb_bf", [128, NCH, T], BF16)
                kb = bsb("kb", [128, NCH, T], BF16)
                vb = bsb("vb", [128, 16, 1040], BF16)
                kob = bsb("kob", [128, NCH, 256], BF16)
                vob = bsb("vob", [128, 2, 1040], BF16)
                nwv = bsb("nwv", [1, 256], BF16)
                nwk = bsb("nwk", [1, 128], BF16)
                nmu_b_bf = bsb("nmu_b_bf", [1, T], BF16)
                wt = bsb("wt", [128, 4, NCH, 128], BF16)
                wvres = bsb("wvres", [128, 2, NCH, 256], BF16)
                qb = bsb("qb", [128, NCH, 256], BF16)
                scr8 = bsb("scr8", [128, NCH, 256], F32)
                yt8 = bsb("yt8", [128, 2, 256], F32)
                r_cb = bsb("r_cb", [128, 16], F32)
                l_cb = bsb("l_cb", [128, 16], F32)
                c_cb = bsb("c_cb", [128, 16], F32)
                btab = bsb("btab", [128, 32], F32)
                btabd = bsb("btabd", [128, 2], F32)
                negb = bsb("negb", [128, 1], F32)
                bscr = bsb("bscr", [1, 8], F32)


                wstage_b = bsb("wstage_b", [128, NCH, 256], BF16)
                maxkf = bsb("maxkf", [128, 32], F32)
                wtuse = [0, 0, 0, 0]
                wtld = [0, 0, 0, 0]
                wvuse = [0, 0]
                wvld = [0, 0]
                wold = [0, 0, 0, 0]
                x2use = [0, 0, 0, 0]
                qbfree = [0]
                qbld = [0]
                stage_st = [0]
                kpre = [False]

                def emit_B1(bb, xbgp):
                    x2v = [kb[:, j, :].rearrange("p (c n) -> p c n", c=8)
                           for j in range(4)]
                    pe_entry = P.cnt["pe"]
                    for j in range(4):
                        x2use[j] = max(x2use[j], pe_entry)
                    with nc.psum_tensor(f"psB_su{bb}", [1, 2048], F32) as psu, \
                         nc.psum_tensor(f"psB_sq{bb}", [1, 2048], F32) as psq:
                        for sl in range(8):
                            k = sl % 4
                            @blk.vector
                            def _(v, sl=sl, k=k, pw=x2use[k], xbgp=xbgp):
                                P.wv(v, "xb", xbgp)
                                P.wv(v, "pe", pw)
                                v.tensor_mul(x2v[k], xb_bf[:, :, sl * 256:(sl + 1) * 256],
                                             xb_bf[:, :, sl * 256:(sl + 1) * 256])
                            sq = P.cnt["dve"]
                            @blk.tensor
                            def _(t, sl=sl, k=k, sq=sq, xbgp=xbgp):
                                P.wv(t, "dve", sq)
                                P.wv(t, "xb", xbgp)
                                for c in range(NCH):
                                    t.matmul(psu[:, sl * 256:(sl + 1) * 256], invd_bf[:],
                                             xb_bf[:, c, sl * 256:(sl + 1) * 256],
                                             start=(c == 0), stop=(c == NCH - 1))
                                    i = t.matmul(psq[:, sl * 256:(sl + 1) * 256], invd_bf[:],
                                                 x2v[k][:, c, :],
                                                 start=(c == 0), stop=(c == NCH - 1))
                                P.inc(i, "pe")
                            x2use[k] = P.cnt["pe"]
                        @blk.vector
                        def _(v):
                            P.w(v, "pe", "gp")
                            v.tensor_copy(rw1[:], psu[:])
                            v.tensor_copy(rw2[:], psq[:])
                            v.tensor_scalar_mul(nmu_b_bf[:], rw1[:], -1.0)

                def emit_B1_tail(bb):
                    with nc.psum_tensor(f"psB_var{bb}", [1, 512], F32) as pvar:
                        @blk.vector
                        def _(v):
                            for sl in range(4):
                                v.tensor_mul(pvar[:], rw1[:, sl * 512:(sl + 1) * 512],
                                             rw1[:, sl * 512:(sl + 1) * 512])
                                v.tensor_sub(rw2[:, sl * 512:(sl + 1) * 512],
                                             rw2[:, sl * 512:(sl + 1) * 512], pvar[:])
                    @blk.scalar
                    def _(sc):
                        P.w(sc, "dve")
                        sc.activation(rw2[:], rw2[:], AF.Sqrt, bias=epsc[:])
                    with nc.psum_tensor(f"psB_tr{bb}", [128, 16], F32) as ptrb:
                        @blk.tensor
                        def _(t):
                            P.w(t, "act")
                            for jt in range(16):
                                i = t.transpose(ptrb[:, jt:jt + 1],
                                                rw2[:, jt * 128:(jt + 1) * 128], idy[:])
                            P.inc(i, "pe")
                        @blk.vector
                        def _(v):
                            P.w(v, "pe")
                            v.reciprocal(r_cb[:], ptrb[:])
                            v.tensor_copy(c_cb[:], ptrb[:])
                            v.tensor_reduce(bscr[:, 1:2], rw2[:], axis=AX.X, op=ALU.min)
                            v.reciprocal(bscr[:, 1:2], bscr[:, 1:2])
                    @blk.scalar
                    def _(sc):
                        P.w(sc, "dve")
                        sc.activation(l_cb[:], r_cb[:], AF.Ln)
                        sc.activation(bscr[:, 2:3], bscr[:, 1:2], AF.Ln)
                    tailact[0] = P.cnt["act"]

                qpark_done = [(0, 0)]
                tailact = [0]
                xbgp_l = [0]
                ystor = [0]
                for b in range(B):
                    if b == 0:
                        @blk.gpsimd
                        def _(g):
                            P.w(g, "pe", "dve")
                            P.inc(g.dma_start(out=xb_bf[:],
                                              in_=xf_r[:, :, 0:T]), "xb", 16)
                        xbgp_l[0] = P.cnt["xb"]
                        qpark_done[0] = (P.cnt["q0"], P.cnt["q1"])
                        emit_B1(0, xbgp_l[0])
                        emit_B1_tail(0)
                        for slot in range(4):
                            @blk.sync
                            def _(s, slot=slot):
                                P.inc(s.dma_start(out=wt[:, slot],
                                                  in_=wqkvbf_r[:, :, 1024 + slot * 128:
                                                               1024 + (slot + 1) * 128]),
                                      f"w{slot}", 16)
                            wtld[slot] = P.cnt[f"w{slot}"]
                        for ft in range(2):
                            @blk.sync
                            def _(s, ft=ft):
                                P.inc(s.dma_start(out=wvres[:, ft],
                                                  in_=wqkvbf_r[:, :, 2048 + ft * 256:
                                                               2048 + (ft + 1) * 256]),
                                      f"v{ft}", 16)
                            wvld[ft] = P.cnt[f"v{ft}"]
                    @blk.sync
                    def _(s, b=b, pw=qbfree[0], sw=qpark_done[0]):
                        P.wv(s, "q0", sw[0])
                        P.wv(s, "q1", sw[1])
                        P.wv(s, "pe", pw)
                        P.inc(s.dma_start(out=qb[:],
                                          in_=qpark_r[:, :, b * 256:(b + 1) * 256]), "qb", 16)
                    qbld[0] = P.cnt["qb"]
                    nwkrdy = [P.cnt["dve"]]
                    pkevac = [P.cnt["dve"], P.cnt["dve"]]
                    pkoevac = [P.cnt["dve"], P.cnt["dve"]]
                    with nc.psum_tensor(f"psB_kA_{b}", [128, 512], F32) as pkA, \
                         nc.psum_tensor(f"psB_kB_{b}", [128, 512], F32) as pkB, \
                         nc.psum_tensor(f"psB_koA_{b}", [128, 256], F32) as pkoA, \
                         nc.psum_tensor(f"psB_koB_{b}", [128, 256], F32) as pkoB:
                        pks = (pkA, pkB)
                        pkos = (pkoA, pkoB)
                        for fi in range(8):
                            slot = fi % 4
                            with nc.psum_tensor(f"psB_wk{b}_{fi}", [1, 128], F32) as pws2:
                                @blk.tensor
                                def _(t, slot=slot, pws2=pws2, lw=wtld[slot], dw=nwkrdy[0]):
                                    P.wv(t, f"w{slot}", lw)
                                    P.wv(t, "dve", dw)
                                    for ec in range(NCH):
                                        i = t.matmul(pws2[:], ones_bf[:], wt[:, slot, ec, :],
                                                     start=(ec == 0), stop=(ec == NCH - 1))
                                    P.inc(i, "pe")
                                wsc = P.cnt["pe"]
                                @blk.vector
                                def _(v, pws2=pws2, wsc=wsc):
                                    P.wv(v, "pe", wsc)
                                    v.tensor_copy(nwk[:], pws2[:])
                                nwkrdy[0] = P.cnt["dve"]
                            for tt in range(4):
                                pk = pks[tt % 2]
                                @blk.tensor
                                def _(t, tt=tt, pk=pk, slot=slot,
                                      dw=max(pkevac[tt % 2], nwkrdy[0]), lw=wtld[slot]):
                                    P.wv(t, f"w{slot}", lw)
                                    P.wv(t, "dve", dw)
                                    for c in range(NCH):
                                        t.matmul(pk[:], wt[:, slot, c, :],
                                                 xb_bf[:, c, tt * 512:(tt + 1) * 512],
                                                 start=(c == 0), stop=False)
                                    i = t.matmul(pk[:], nwk[:],
                                                 nmu_b_bf[:, tt * 512:(tt + 1) * 512],
                                                 start=False, stop=True)
                                    P.inc(i, "pe")
                                kch = P.cnt["pe"]
                                @blk.vector
                                def _(v, fi=fi, tt=tt, pk=pk, kch=kch):
                                    P.wv(v, "pe", kch)
                                    v.tensor_copy(kb[:, fi, tt * 512:(tt + 1) * 512], pk[:])
                                    v.tensor_reduce(maxkf[:, fi * 4 + tt:fi * 4 + tt + 1],
                                                    pk[:], axis=AX.X, op=ALU.max,
                                                    apply_absolute_value=True)
                                pkevac[tt % 2] = P.cnt["dve"]
                            pko = pkos[fi % 2]
                            @blk.tensor
                            def _(t, b=b, pko=pko, slot=slot,
                                  dw=max(pkoevac[fi % 2], nwkrdy[0])):
                                P.wv(t, "dve", dw)
                                for c in range(NCH):
                                    t.matmul(pko[:], wt[:, slot, c, :],
                                             xa_bf[:, c, b * 256:(b + 1) * 256],
                                             start=(c == 0), stop=False)
                                i = t.matmul(pko[:], nwk[:],
                                             nmu_own_bf[:, b * 256:(b + 1) * 256],
                                             start=False, stop=True)
                                P.inc(i, "pe")
                            koch = P.cnt["pe"]
                            @blk.vector
                            def _(v, fi=fi, pko=pko, koch=koch):
                                P.wv(v, "pe", koch)
                                v.tensor_copy(kob[:, fi, :], pko[:])
                            pkoevac[fi % 2] = P.cnt["dve"]
                            wtuse[slot] = P.cnt["pe"]
                            if fi < 4:
                                @blk.sync
                                def _(s, fi=fi, slot=slot, pw=wtuse[slot]):
                                    P.wv(s, "pe", pw)
                                    P.inc(s.dma_start(out=wt[:, slot],
                                                      in_=wqkvbf_r[:, :, 1024 + (fi + 4) * 128:
                                                                   1024 + (fi + 5) * 128]),
                                          f"w{slot}", 16)
                                wtld[slot] = P.cnt[f"w{slot}"]
                    if b > 0:
                        emit_B1_tail(b)
                    nwvrdy = [P.cnt["dve"]]
                    pvevac = [P.cnt["dve"], P.cnt["dve"]]
                    with nc.psum_tensor(f"psB_v0_{b}", [128, 256], F32) as pv0, \
                         nc.psum_tensor(f"psB_v1_{b}", [128, 256], F32) as pv1, \
                         nc.psum_tensor(f"psB_wsv{b}", [1, 256], F32) as pwsv:
                        pvs = (pv0, pv1)
                        for ft in range(4):
                            slot = ft % 2
                            @blk.tensor
                            def _(t, slot=slot, lw=wvld[slot], dw=nwvrdy[0]):
                                P.wv(t, f"v{slot}", lw)
                                P.wv(t, "dve", dw)
                                for ec in range(NCH):
                                    i = t.matmul(pwsv[:], ones_bf[:], wvres[:, slot, ec, :],
                                                 start=(ec == 0), stop=(ec == NCH - 1))
                                P.inc(i, "pe")
                            wsc2 = P.cnt["pe"]
                            @blk.vector
                            def _(v, wsc2=wsc2):
                                P.wv(v, "pe", wsc2)
                                v.tensor_copy(nwv[:], pwsv[:])
                            nwvrdy[0] = P.cnt["dve"]
                            for jt in range(16):
                                pv = pvs[jt % 2]
                                @blk.tensor
                                def _(t, jt=jt, slot=slot, pv=pv,
                                      dw=max(pvevac[jt % 2], nwvrdy[0])):
                                    P.wv(t, "dve", dw)
                                    for ec in range(NCH):
                                        t.matmul(pv[:],
                                                 xb_bf[:, ec, jt * 128:(jt + 1) * 128],
                                                 wvres[:, slot, ec, :],
                                                 start=(ec == 0), stop=False)
                                    i = t.matmul(pv[:],
                                                 nmu_b_bf[:, jt * 128:(jt + 1) * 128],
                                                 nwv[:],
                                                 start=False, stop=True)
                                    P.inc(i, "pe")
                                vch = P.cnt["pe"]
                                @blk.vector
                                def _(v, jt=jt, ft=ft, pv=pv, vch=vch):
                                    P.wv(v, "pe", vch)
                                    dst = vb[:].rearrange("p jt (h e) -> p jt h e", h=16)[
                                        :, jt, ft * 4:(ft + 1) * 4, 0:64]
                                    v.tensor_copy(dst, pv[:].rearrange("p (h e) -> p h e", h=4))
                                pvevac[jt % 2] = P.cnt["dve"]
                            for sj in range(2):
                                pv = pvs[sj]
                                @blk.tensor
                                def _(t, sj=sj, slot=slot, b=b, pv=pv,
                                      dw=max(pvevac[sj], nwvrdy[0])):
                                    P.wv(t, "dve", dw)
                                    for ec in range(NCH):
                                        t.matmul(pv[:],
                                                 xa_bf[:, ec, b * 256 + sj * 128:
                                                        b * 256 + (sj + 1) * 128],
                                                 wvres[:, slot, ec, :],
                                                 start=(ec == 0), stop=False)
                                    i = t.matmul(pv[:],
                                                 nmu_own_bf[:, b * 256 + sj * 128:
                                                            b * 256 + (sj + 1) * 128],
                                                 nwv[:],
                                                 start=False, stop=True)
                                    P.inc(i, "pe")
                                vch = P.cnt["pe"]
                                @blk.vector
                                def _(v, sj=sj, ft=ft, pv=pv, vch=vch):
                                    P.wv(v, "pe", vch)
                                    dst = vob[:].rearrange("p jt (h e) -> p jt h e", h=16)[
                                        :, sj, ft * 4:(ft + 1) * 4, 0:64]
                                    v.tensor_copy(dst, pv[:].rearrange("p (h e) -> p h e", h=4))
                                pvevac[sj] = P.cnt["dve"]
                            wvuse[slot] = P.cnt["pe"]
                            if ft < 2:
                                @blk.sync
                                def _(s, ft=ft, slot=slot, pw=wvuse[slot]):
                                    P.wv(s, "pe", pw)
                                    P.inc(s.dma_start(
                                        out=wvres[:, slot],
                                        in_=wqkvbf_r[:, :, 2048 + (ft + 2) * 256:
                                                     2048 + (ft + 3) * 256]), f"v{slot}", 16)
                                wvld[slot] = P.cnt[f"v{slot}"]
                        for jt in range(16):
                            @blk.vector
                            def _(v, jt=jt):
                                dst = vb[:].rearrange("p jt (h e) -> p jt h e", h=16)[:, jt, :, 64:65]
                                src2 = bass.AP(c_cb, jt, [list(c_cb.ap().ap[0]), [0, 16], [1, 1]])
                                v.tensor_copy(dst, src2)
                        for sj in range(2):
                            @blk.vector
                            def _(v, sj=sj, b=b):
                                dst = vob[:].rearrange("p jt (h e) -> p jt h e", h=16)[:, sj, :, 64:65]
                                src2 = bass.AP(co_c, b * 2 + sj, [list(co_c.ap().ap[0]), [0, 16], [1, 1]])
                                v.tensor_copy(dst, src2)

                    if b + 1 < B:
                        @blk.gpsimd
                        def _(g, b=b):
                            P.w(g, "pe", "dve")
                            P.inc(g.dma_start(out=xb_bf[:],
                                              in_=xf_r[:, :, (b + 1) * T:(b + 2) * T]),
                                  "xb", 16)
                        xbgp_l[0] = P.cnt["xb"]
                    @blk.vector
                    def _(v):
                        P.inc(v.tensor_reduce(
                            negb[:], maxkf[:],
                            axis=AX.X, op=ALU.max, apply_absolute_value=True), "dve")
                    with nc.psum_tensor(f"psB_km{b}", [1, 128], F32) as pkm, \
                         nc.psum_tensor(f"psB_nb{b}", [128, 1], F32) as pnb:
                        @blk.tensor
                        def _(t, pkm=pkm):
                            P.w(t, "dve")
                            P.inc(t.transpose(pkm[:], negb[:], iden[:]), "pe")
                        @blk.vector
                        def _(v, pkm=pkm):
                            P.w(v, "pe")
                            v.tensor_reduce(bscr[:, 0:1], pkm[:], axis=AX.X, op=ALU.max)
                            P.inc(v.tensor_mul(bscr[:, 0:1], bscr[:, 0:1], bscr[:, 1:2]), "dve")
                        @blk.vector
                        def _(v, aw=None):
                            P.wv(v, "act", tailact[0])
                            v.tensor_mul(bscr[:, 0:1], bscr[:, 0:1], qmax2[:])
                            v.tensor_scalar_mul(bscr[:, 0:1], bscr[:, 0:1], 8.0)
                            v.tensor_add(bscr[:, 0:1], bscr[:, 0:1], bscr[:, 2:3])
                            P.inc(v.tensor_scalar_mul(bscr[:, 0:1], bscr[:, 0:1], -1.0), "dve")
                        @blk.tensor
                        def _(t, pnb=pnb):
                            P.w(t, "dve")
                            P.inc(t.matmul(pnb[:], ones_f1[:], bscr[:, 0:1],
                                           start=True, stop=True), "pe")
                        @blk.vector
                        def _(v, pnb=pnb):
                            P.w(v, "pe")
                            P.inc(v.tensor_copy(negb[:], pnb[:]), "dve")
                    @blk.vector
                    def _(v, b=b, aw=tailact[0]):
                        P.w(v, "dve")
                        P.wv(v, "act", aw)
                        lview = bass.AP(l_cb, 0, [list(l_cb.ap().ap[0]), [0, 2], [1, 16]])
                        v.tensor_add(btab[:].rearrange("p (s k) -> p s k", s=2), lview,
                                     deny[:].rearrange("p (s k) -> p s k", s=2))
                        v.tensor_scalar_add(btab[:], btab[:], negb[:])
                        v.tensor_copy(btabd[:], lo_c[:, 2 * b:2 * b + 2])
                        P.inc(v.tensor_scalar_add(btabd[:], btabd[:], negb[:]), "dve")
                    @blk.tensor
                    def _(t, lw=qbld[0]):
                        P.wv(t, "qb", lw)
                    @blk.scalar
                    def _(sc):
                        P.w(sc, "dve", "pool")


                    with ExitStack() as sctx:
                        ssb = lambda n, sh, dt: sctx.enter_context(nc.sbuf_tensor(n, sh, dt))
                        psb = [ssb(f"psb{se}_{b}", [128, 1024], BF16) for se in range(2)]
                        onn = ssb(f"onn_{b}", [128, NCH, 256], BF16)
                        rec0, rec1 = rw1, rw2
                        recs = (rec0, rec1)
                        with nc.psum_tensor(f"psB_scA_{b}", [128, 1024], F32) as scA, \
                             nc.psum_tensor(f"psB_scB_{b}", [128, 1024], F32) as scB, \
                             nc.psum_tensor(f"psB_av00_{b}", [128, 512], F32) as av00, \
                             nc.psum_tensor(f"psB_av01_{b}", [128, 512], F32) as av01, \
                             nc.psum_tensor(f"psB_av10_{b}", [128, 512], F32) as av10, \
                             nc.psum_tensor(f"psB_av11_{b}", [128, 512], F32) as av11:
                            sc = (scA, scB)
                            av = ((av00, av01), (av10, av11))
                            expcnt = [0, 0]
                            avevcnt = [(P.cnt["dve"], P.cnt["pool"]),
                                       (P.cnt["dve"], P.cnt["pool"])]
                            tricnt = [P.cnt["dve"]]
                            pending = [None]
                            slot = 0
                            for hpg in range(2):
                                hps = [4 * hpg + i_ for i_ in range(4)]
                                for s_ in range(2):
                                    g2 = slot % 2
                                    units = [(True, 0)] + [(False, k) for k in range(8 * (s_ + 1))]
                                    nu = len(units)
                                    pe_base = P.cnt["pe"]
                                    act_base = P.cnt["act"]
                                    dve_tri = P.cnt["dve"] + 1
                                    @blk.scalar
                                    def _(scl_, units=units, s_=s_, b=b, pe_base=pe_base,
                                          dve_tri=dve_tri):
                                        for u, (isdiag, k) in enumerate(units):
                                            p2, r2 = divmod(u, 2)
                                            P.wv(scl_, "pe", pe_base + 4 * p2 + 1 + r2)
                                            if isdiag:
                                                P.wv(scl_, "dve", dve_tri)
                                                scalev = ro_c[:, 2 * b + s_: 2 * b + s_ + 1]
                                                bia = btabd[:, s_:s_ + 1]
                                            else:
                                                scalev = r_cb[:, k:k + 1]
                                                bia = btab[:, s_ * 16 + k: s_ * 16 + k + 1]
                                            se = u % 2
                                            scl_.activation(psb[se][:], sc[se][:],
                                                            AF.Exp, bias=bia,
                                                            scale=scalev)
                                    assert P.cnt["act"] == act_base + nu
                                    ui = 0
                                    first_pair = True
                                    while ui < nu:
                                        pu = [(ui + j, units[ui + j][0], units[ui + j][1],
                                               (ui + j) % 2) for j in range(min(2, nu - ui))]
                                        for (uj, isdiag, k, se) in pu:
                                            @blk.tensor
                                            def _(t, s_=s_, isdiag=isdiag, k=k, uj=uj, se=se,
                                                  hps=hps):
                                                P.wv(t, "act", expcnt[se])
                                                if uj == 0:
                                                    P.wv(t, "dve", tricnt[0])
                                                for hi, hp in enumerate(hps):
                                                    lt = (kob[:, hp, s_ * 128:(s_ + 1) * 128]
                                                          if isdiag
                                                          else kb[:, hp, k * 128:(k + 1) * 128])
                                                    t.matmul(sc[se][:, hi * 128:(hi + 1) * 128],
                                                             lt[0:64, :],
                                                             qb[0:64, hp, s_ * 128:(s_ + 1) * 128],
                                                             start=True, stop=True,
                                                             tile_position=(0, 0))
                                                    i = t.matmul(
                                                        sc[se][:, 512 + hi * 128:512 + (hi + 1) * 128],
                                                        lt[64:128, :],
                                                        qb[64:128, hp, s_ * 128:(s_ + 1) * 128],
                                                        start=True, stop=True,
                                                        tile_position=(64, 0))
                                                P.inc(i, "pe")
                                            assert P.cnt["pe"] == pe_base + 4 * (uj // 2) + 1 + uj % 2
                                            if isdiag:
                                                @blk.vector
                                                def _(v, se=se):
                                                    P.w(v, "pe")
                                                    trib8 = bass.AP(tri, 0,
                                                                    [list(tri.ap().ap[0]),
                                                                     [0, 8], [1, 128]])
                                                    v.tensor_add(
                                                        sc[se][:].rearrange(
                                                            "p (hp i) -> p hp i", hp=8),
                                                        sc[se][:].rearrange(
                                                            "p (hp i) -> p hp i", hp=8),
                                                        trib8)
                                                tricnt[0] = P.cnt["dve"]
                                                assert tricnt[0] == dve_tri
                                        for (uj, isdiag, k, se) in pu:
                                            @blk.tensor
                                            def _(t, s_=s_, isdiag=isdiag, k=k, uj=uj, se=se,
                                                  hps=hps, g2=g2,
                                                  ecnt=act_base + (uj + 1),
                                                  last_u=(uj == nu - 1)):
                                                P.wv(t, "act", ecnt)
                                                if uj == 0:
                                                    P.wv(t, "dve", avevcnt[g2][0])
                                                    P.wv(t, "pool", avevcnt[g2][1])
                                                for h01 in range(2):
                                                    for hi, hp in enumerate(hps):
                                                        h = 2 * hp + h01
                                                        vt = (vob[:, s_, h * 65:(h + 1) * 65]
                                                              if isdiag
                                                              else vb[:, k, h * 65:(h + 1) * 65])
                                                        i = t.matmul(
                                                            av[g2][h01][0:65,
                                                                        hi * 128:(hi + 1) * 128],
                                                            vt,
                                                            psb[se][:, h01 * 512 + hi * 128:
                                                                    h01 * 512 + (hi + 1) * 128],
                                                            start=(uj == 0 and hi == 0),
                                                            stop=last_u)
                                                P.inc(i, "pe")
                                        ui += len(pu)
                                        if first_pair:
                                            first_pair = False
                                            if pending[0] is not None:
                                                pending[0]()
                                                pending[0] = None
                                    last_even = nu - 1 if (nu - 1) % 2 == 0 else nu - 2
                                    last_odd = nu - 1 if (nu - 1) % 2 == 1 else nu - 2
                                    expcnt[0] = act_base + (last_even + 1)
                                    expcnt[1] = act_base + (last_odd + 1)
                                    def _evac(s_=s_, hpg=hpg, g2=g2):
                                        @blk.vector
                                        def _(v):
                                            P.w(v, "pe")
                                            for h01 in range(2):
                                                dst = onn[h01 * 64:(h01 + 1) * 64,
                                                            4 * hpg:4 * hpg + 4,
                                                            s_ * 128:(s_ + 1) * 128]
                                                v.tensor_copy(dst,
                                                              av[g2][h01][0:64, :].rearrange(
                                                                  "p (c n) -> p c n", c=4))
                                        @blk.vector
                                        def _(v):
                                            for h01 in range(2):
                                                rdst = bass.AP(recs[h01],
                                                               4 * hpg * 256 + s_ * 128,
                                                               [list(recs[h01].ap().ap[0]),
                                                                [256, 4], [1, 128]])
                                                v.tensor_copy(rdst, av[g2][h01][64:65, :]
                                                              .rearrange("p (c n) -> p c n",
                                                                         c=4))
                                        avevcnt[g2] = (P.cnt["dve"], P.cnt["pool"])
                                        @blk.gpsimd
                                        def _(g):
                                            P.w(g, "dve")
                                            for h01 in range(2):
                                                wdst = d_recb.ap()[h01:h01 + 1, :].rearrange(
                                                    "a (c u n) -> a c u n", c=8, u=2)[
                                                    :, 4 * hpg:4 * hpg + 4, s_, :]
                                                wsrc = bass.AP(recs[h01],
                                                               4 * hpg * 256 + s_ * 128,
                                                               [list(recs[h01].ap().ap[0]),
                                                                [256, 4], [1, 128]])
                                                P.inc(g.dma_start(out=wdst, in_=wsrc),
                                                      "gp", 16)
                                            g.wait_ge(P.sem["gp"], P.cnt["gp"])
                                            for h01 in range(2):
                                                bsrc = bass.AP(
                                                    d_recb,
                                                    h01 * 2048 + 4 * hpg * 256 + s_ * 128,
                                                    [[0, 64], [256, 4], [1, 128]])
                                                bdst = scr8[h01 * 64:(h01 + 1) * 64,
                                                            4 * hpg:4 * hpg + 4,
                                                            s_ * 128:(s_ + 1) * 128]
                                                P.inc(g.dma_start(out=bdst, in_=bsrc),
                                                      "gp", 16)
                                        @blk.vector
                                        def _(v):
                                            P.w(v, "gp")
                                            for h01 in range(2):
                                                sv = scr8[h01 * 64:(h01 + 1) * 64,
                                                          4 * hpg:4 * hpg + 4,
                                                          s_ * 128:(s_ + 1) * 128]
                                                v.reciprocal(sv, sv)
                                                onnv = onn[h01 * 64:(h01 + 1) * 64,
                                                           4 * hpg:4 * hpg + 4,
                                                           s_ * 128:(s_ + 1) * 128]
                                                v.tensor_mul(onnv, onnv, sv)
                                    pending[0] = _evac
                                    slot += 1
                            if pending[0] is not None:
                                pending[0]()
                                pending[0] = None
                        actB6 = P.cnt["act"]
                        qbfree[0] = P.cnt["pe"]
                        factor_gp = P.cnt["gp"]
                        for fc in range(4):
                            @blk.sync
                            def _(s, fc=fc, pw=wtuse[fc]):
                                P.wv(s, "pe", pw)
                                P.inc(s.dma_start(out=wt[:, fc],
                                                  in_=woutbf_r[:, :, fc * 128:(fc + 1) * 128]),
                                      f"w{fc}", 16)
                            wold[fc] = P.cnt[f"w{fc}"]
                        mulcnt = P.cnt["dve"]
                        if debug and b == 1:
                            @blk.sync
                            def _(s):
                                P.w(s, "dve")
                                P.inc(s.dma_start(out=d_ton,
                                                  in_=onn[:].rearrange("p c n -> p (c n)")),
                                      "st", 16)
                                P.inc(s.dma_start(out=d_tor,
                                                  in_=kob[:].rearrange("p c n -> p (c n)")),
                                      "st", 16)
                                P.inc(s.dma_start(out=d_tyt,
                                                  in_=vob[:, 0, 0:256]), "st", 16)
                                P.inc(s.dma_start(out=d_tpy,
                                                  in_=vob[:, 0, 256:512]), "st", 16)
                                P.inc(s.dma_start(out=d_tfa,
                                                  in_=scr8[:].rearrange("p c n -> p (c n)")),
                                      "st", 16)
                                P.inc(s.dma_start(out=d_trc[0:1, 0:2048], in_=bass.AP(ro_c, 0, [[16, 128], [1, 16]]).rearrange("p c -> (p c)")[None, :] if False else rec0[:]), "st", 16)
                                P.inc(s.dma_start(out=d_trc[1:2, :], in_=rec1[:]), "st", 16)
                                P.inc(s.dma_start(out=d_twt[:, 0:16], in_=ro_c[:]), "st", 16)
                                P.inc(s.dma_start(out=d_twt[:, 16:32], in_=c_cb[:]), "st", 16)
                                P.inc(s.dma_start(out=d_twt[:, 32:48], in_=co_c[:]), "st", 16)
                                P.inc(s.dma_start(out=d_twt[:, 48:64], in_=r_cb[:]), "st", 16)
                                s.wait_ge(P.sem["st"], P.cnt["st"])
                        @blk.sync
                        def _(s, b=b, mulcnt=mulcnt):
                            P.wv(s, "dve", mulcnt)
                            P.inc(s.dma_start(out=scr8[:],
                                              in_=xo_r[:, :, b * 256:(b + 1) * 256]), "rs", 16)
                        scr8ld = P.cnt["rs"]
                        with nc.psum_tensor(f"psB_y0_{b}", [128, 256], F32) as py0, \
                             nc.psum_tensor(f"psB_y1_{b}", [128, 256], F32) as py1:
                            pys = (py0, py1)
                            pyfree = [mulcnt, mulcnt]
                            for fc in range(8):
                                slot4 = fc % 4
                                py = pys[fc % 2]
                                @blk.tensor
                                def _(t, fc=fc, py=py, slot4=slot4, actB6=actB6,
                                      lw=wold[slot4], dw=max(mulcnt, pyfree[fc % 2])):
                                    P.wv(t, f"w{slot4}", lw)
                                    P.wv(t, "dve", dw)
                                    if fc == 0:
                                        P.wv(t, "act", actB6)
                                    for ec in range(NCH):
                                        i = t.matmul(py[:], wt[:, slot4, ec, :], onn[:, ec, :],
                                                     start=(ec == 0), stop=(ec == NCH - 1))
                                    P.inc(i, "pe")
                                ych = P.cnt["pe"]
                                @blk.vector
                                def _(v, fc=fc, py=py, ych=ych, sl=scr8ld, sw=ystor[0]):
                                    P.wv(v, "pe", ych)
                                    P.wv(v, "rs", sl)
                                    P.wv(v, "yp", sw)
                                    v.tensor_add(yt8[:, fc % 2, :], py[:], scr8[:, fc, :])
                                pyfree[fc % 2] = P.cnt["dve"]
                                wtuse[slot4] = P.cnt["pe"]
                                if fc < 4:
                                    @blk.sync
                                    def _(s, fc=fc, slot4=slot4, pw=wtuse[slot4]):
                                        P.wv(s, "pe", pw)
                                        P.inc(s.dma_start(
                                            out=wt[:, slot4],
                                            in_=woutbf_r[:, :, (fc + 4) * 128:(fc + 5) * 128]),
                                            f"w{slot4}", 16)
                                    wold[slot4] = P.cnt[f"w{slot4}"]
                                if fc % 2 == 1:
                                    if debug and b == 7 and fc == 1:
                                        @blk.sync
                                        def _(s, dw=max(pyfree[0], pyfree[1])):
                                            P.wv(s, "dve", dw)
                                            P.inc(s.dma_start(
                                                out=d_tb7[:, 4096:4608],
                                                in_=yt8[:].rearrange("p a n -> p (a n)")),
                                                "st", 16)
                                            P.inc(s.dma_start(
                                                out=d_tb7[:, 2048:4096],
                                                in_=scr8[:].rearrange("p c n -> p (c n)")),
                                                "st", 16)
                                        @blk.sync
                                        def _(s):
                                            P.inc(s.dma_start(
                                                out=d_tz[:, 0:2048],
                                                in_=onn[:].rearrange("p c n -> p (c n)")),
                                                "st", 16)
                                    @blk.sync
                                    def _(s, fc=fc, b=b, dw=max(pyfree[0], pyfree[1])):
                                        P.wv(s, "dve", dw)
                                        P.inc(s.dma_start(
                                            out=d_ypark.ap().rearrange("(c p) n -> p c n", p=128)[
                                                :, fc - 1:fc + 1, b * 256:(b + 1) * 256],
                                            in_=yt8[:]), "yp", 16)
                                    ystor[0] = P.cnt["yp"]
                            if b + 1 < B:
                                for fi in range(4):
                                    @blk.sync
                                    def _(s, fi=fi, pw=wtuse[fi]):
                                        P.wv(s, "pe", pw)
                                        P.inc(s.dma_start(
                                            out=wt[:, fi],
                                            in_=wqkvbf_r[:, :, 1024 + fi * 128:
                                                         1024 + (fi + 1) * 128]), f"w{fi}", 16)
                                    wtld[fi] = P.cnt[f"w{fi}"]
                                for ft in range(2):
                                    @blk.sync
                                    def _(s, ft=ft, pw=wvuse[ft]):
                                        P.wv(s, "pe", pw)
                                        P.inc(s.dma_start(
                                            out=wvres[:, ft],
                                            in_=wqkvbf_r[:, :, 2048 + ft * 256:
                                                         2048 + (ft + 1) * 256]), f"v{ft}", 16)
                                    wvld[ft] = P.cnt[f"v{ft}"]
                        if b + 1 < B:
                            emit_B1(b + 1, xbgp_l[0])
                    w8 = FFN_ORDER[b]
                    for q4 in range(4):
                        @blk.gpsimd
                        def _(g, w8=w8, q4=q4, sw=stage_st[0]):
                            P.wv(g, "sg", sw)
                            P.inc(g.dma_start(out=wstage_b[:],
                                              in_=w_r[w8][:, :, q4 * 256:(q4 + 1) * 256]),
                                  "c1", 16)
                        stage_gp = P.cnt["c1"]
                        @blk.sync
                        def _(s, b=b, q4=q4, gc=stage_gp):
                            P.wv(s, "c1", gc)
                            P.inc(s.dma_start(
                                out=wffnbf_r[:, :, b * 1024 + q4 * 256:
                                             b * 1024 + (q4 + 1) * 256],
                                in_=wstage_b[:]), "sg", 16)
                        stage_st[0] = P.cnt["sg"]
                    if debug and b == 0:
                        @blk.sync
                        def _(s):
                            P.inc(s.dma_start(out=d_tkb, in_=kb[:].rearrange("p c n -> p (c n)")),
                                  "st", 16)
                            P.inc(s.dma_start(out=d_tnb, in_=negb[:]), "st", 16)
                            s.wait_ge(P.sem["st"], P.cnt["st"])
        ypark_r = d_ypark.ap().rearrange("(c p) n -> p c n", p=128)

        with ExitStack() as cctx:
            csb = lambda n, sh, dt: cctx.enter_context(nc.sbuf_tensor(n, sh, dt))
            z = csb("z", [128, NCH, OWN], BF16)
            cuo = csb("cuo", [128, NCH, 256], F32)
            wt_c = csb("wt_c", [128, 2, NCH, 128], BF16)
            ysl = csb("ysl", [128, 2, NCH, 256], F32)
            tmp_c = csb("tmp_c", [128, 2, 256], F32)
            ybf_v = [z[:, :, 0:256], z[:, :, 256:512]]
            ysq_v = [z[:, :, 512:768], z[:, :, 768:1024]]

            yslld = [0, 0]
            ysluse = [0, 0]
            ystuse = [0, 0]
            ypdone = P.cnt["yp"]
            with nc.psum_tensor("psC_su", [1, 2048], F32) as psu, \
                 nc.psum_tensor("psC_sq", [1, 2048], F32) as psq:
                for sl in range(8):
                    k = sl % 2
                    @blk.sync
                    def _(s, sl=sl, k=k, du=ysluse[k], sw=ypdone):
                        P.wv(s, "dve", du)
                        P.wv(s, "yp", sw)
                        P.inc(s.dma_start(out=ysl[:, k],
                                          in_=ypark_r[:, :, sl * 256:(sl + 1) * 256]), f"y{k}", 16)
                    yslld[k] = P.cnt[f"y{k}"]
                    @blk.vector
                    def _(v, k=k, lw=yslld[k], pw=ystuse[k]):
                        P.wv(v, f"y{k}", lw)
                        P.wv(v, "pe", pw)
                        v.tensor_copy(ybf_v[k], ysl[:, k])
                        v.tensor_mul(ysq_v[k], ysl[:, k], ysl[:, k])
                    ysluse[k] = P.cnt["dve"]
                    @blk.tensor
                    def _(t, sl=sl, k=k, dw=ysluse[k]):
                        P.wv(t, "dve", dw)
                        for c in range(NCH):
                            t.matmul(psu[:, sl * 256:(sl + 1) * 256], invd_bf[:],
                                     ybf_v[k][:, c, :], start=(c == 0), stop=(c == NCH - 1))
                            i = t.matmul(psq[:, sl * 256:(sl + 1) * 256], invd_bf[:],
                                         ysq_v[k][:, c, :], start=(c == 0), stop=(c == NCH - 1))
                        P.inc(i, "pe")
                    ystuse[k] = P.cnt["pe"]
                @blk.vector
                def _(v):
                    P.w(v, "pe")
                    v.tensor_copy(rw1[:], psu[:])
                    v.tensor_copy(rw2[:], psq[:])
            @blk.vector
            def _(v):
                for sl in range(8):
                    v.tensor_mul(tmp_c[0:1, 0, 0:256], rw1[:, sl * 256:(sl + 1) * 256],
                                 rw1[:, sl * 256:(sl + 1) * 256])
                    v.tensor_sub(rw2[:, sl * 256:(sl + 1) * 256],
                                 rw2[:, sl * 256:(sl + 1) * 256], tmp_c[0:1, 0, 0:256])
            @blk.scalar
            def _(sc):
                P.w(sc, "dve")
                sc.activation(rw2[:], rw2[:], AF.Sqrt, bias=epsc[:])
            @blk.vector
            def _(v):
                P.w(v, "act")
                v.reciprocal(rw2[:], rw2[:])
            @blk.gpsimd
            def _(g):
                P.w(g, "act", "dve", "gp")
                P.inc(g.dma_start(out=d_rowb.ap(), in_=rw1[:]), "gp", 16)
                g.wait_ge(P.sem["gp"], P.cnt["gp"])
                P.inc(g.dma_start(out=d_recb.ap()[0:1, :], in_=rw2[:]), "gp", 16)
            with nc.sbuf_tensor("mu2bc", [128, OWN], F32) as mu2bc, \
                 nc.sbuf_tensor("r2bc", [128, OWN], F32) as r2bc:
                @blk.gpsimd
                def _(g):
                    P.w(g, "gp")
                    P.inc(g.dma_start(out=mu2bc[:],
                                      in_=d_rowb.ap().partition_broadcast(128)[:, 0, :]),
                          "gp", 16)
                    P.inc(g.dma_start(out=r2bc[:],
                                      in_=bass.AP(d_recb, 0, [[0, 128], [1, OWN]])), "gp", 16)
                zbgp = P.cnt["gp"]
                for sl in range(8):
                    k = sl % 2
                    @blk.sync
                    def _(s, sl=sl, k=k, du=ysluse[k], sw=ypdone):
                        P.wv(s, "dve", du)
                        P.wv(s, "yp", sw)
                        P.inc(s.dma_start(out=ysl[:, k],
                                          in_=ypark_r[:, :, sl * 256:(sl + 1) * 256]), f"y{k}", 16)
                    yslld[k] = P.cnt[f"y{k}"]
                    @blk.vector
                    def _(v, sl=sl, k=k, lw=yslld[k], gw=zbgp):
                        P.wv(v, "gp", gw)
                        P.wv(v, f"y{k}", lw)
                        mub = bass.AP(mu2bc, sl * 256, [list(mu2bc.ap().ap[0]), [0, NCH], [1, 256]])
                        rbb = bass.AP(r2bc, sl * 256, [list(r2bc.ap().ap[0]), [0, NCH], [1, 256]])
                        v.tensor_sub(ysl[:, k], ysl[:, k], mub)
                        v.tensor_mul(z[:, :, sl * 256:(sl + 1) * 256], ysl[:, k], rbb)
                    ysluse[k] = P.cnt["dve"]
            cst = csb("cst", [128, NCH, 256], F32)
            cbf = csb("cbf", [128, NCH, 256], BF16)
            gates = {"f": csb("g_f", [128, NCH, 256], F32),
                     "i": ysl[:, 0], "c": ysl[:, 1]}
            rctx = ExitStack()
            wres = {w: rctx.enter_context(nc.sbuf_tensor("w_" + w, [128, NCH, D], BF16))
                    for w in ("wf", "wi", "wc", "uf", "ui", "uc")}

            @blk.sync
            def _(s):
                P.w(s, "dve")
                for wi_, w8 in enumerate(("wf", "wi", "wc", "uf", "ui", "uc")):
                    P.inc(s.dma_start(out=wres[w8][:],
                                      in_=wffnbf_r[:, :, wi_ * 1024:(wi_ + 1) * 1024]),
                          "c0", 16)
            for c in range(NCH):
                @blk.vector
                def _(v, c=c):
                    P.w(v, "gp")
                    src = bass.AP(bv["incv"], c, [list(bv["incv"].ap().ap[0]), [0, 256]])
                    P.inc(v.tensor_copy(cst[:, c, :], src), "dve")
            @blk.vector
            def _(v):
                P.w(v, "dve", "pool")
                P.inc(v.tensor_copy(cbf[:], cst[:]), "dve")
            gnames = (("f", "wf", "uf", "bfv", AF.Sigmoid),
                      ("i", "wi", "ui", "biv", AF.Sigmoid),
                      ("c", "wc", "uc", "bcv", AF.Tanh))
            with nc.psum_tensor("psC_g0", [128, 256], F32) as pg0, \
                 nc.psum_tensor("psC_g1", [128, 256], F32) as pg1, \
                 nc.psum_tensor("psC_g2", [128, 256], F32) as pg2, \
                 nc.psum_tensor("psC_g3", [128, 256], F32) as pg3:
                pgs = (pg0, pg1, pg2, pg3)
                pguse = [0, 0, 0, 0]
                gi = 0
                for t_ in range(B):
                    for (gn, wn, un, bn, af) in gnames:
                        act_g = P.cnt["act"]
                        chc = []
                        bis = []
                        for mc in range(NCH):
                            bi = gi % 4
                            gi += 1
                            bis.append(bi)
                            pg = pgs[bi]
                            aw = pguse[bi] if mc < 4 else act_g + mc - 3
                            @blk.tensor
                            def _(t, t_=t_, wn=wn, un=un, mc=mc, pg=pg, aw=aw,
                                  first=(mc == 0 and gn == "f")):
                                P.wv(t, "act", aw)
                                if first:
                                    P.w(t, "dve", "pool", "c0")
                                for ec in range(NCH):
                                    t.matmul(pg[:], wres[wn][:, ec, mc * 128:(mc + 1) * 128],
                                             z[:, ec, t_ * 256:(t_ + 1) * 256],
                                             start=(ec == 0), stop=False)
                                for ec in range(NCH):
                                    i = t.matmul(pg[:], wres[un][:, ec, mc * 128:(mc + 1) * 128],
                                                 cbf[:, ec, :],
                                                 start=False, stop=(ec == NCH - 1))
                                P.inc(i, "pe")
                            chc.append(P.cnt["pe"])
                        @blk.scalar
                        def _(scg, gn=gn, bn=bn, af=af, chc=tuple(chc), bis=tuple(bis)):
                            for mc in range(NCH):
                                P.wv(scg, "pe", chc[mc])
                                scg.activation(gates[gn][:, mc, :], pgs[bis[mc]][:], af,
                                               bias=bv[bn][:, mc:mc + 1])
                        assert P.cnt["act"] == act_g + NCH
                        for i_, bi in enumerate(bis):
                            pguse[bi] = act_g + i_ + 1
                    @blk.vector
                    def _(v, t_=t_):
                        P.w(v, "act")
                        v.tensor_mul(cst[:], cst[:], gates["f"][:])
                        v.tensor_mul(gates["f"][:], gates["i"][:], gates["c"][:])
                        v.tensor_add(cst[:], cst[:], gates["f"][:])
                        v.tensor_copy(cbf[:], cst[:])
            rctx.close()
            with ExitStack() as octx:
                wo_res = octx.enter_context(nc.sbuf_tensor("wo_res", [128, NCH, D], BF16))
                ystage = octx.enter_context(nc.sbuf_tensor("ystage", [128, NCH, 256], F32))
                yo_r = d_yo.rearrange("(c p) n -> p c n", p=128)
                gatedone = P.cnt["pe"]
                @blk.sync
                def _(s, pw=gatedone):
                    P.wv(s, "pe", pw)
                    P.inc(s.dma_start(out=wo_res[:],
                                      in_=wffnbf_r[:, :, 6 * 1024:7 * 1024]), "c0", 16)
                wo_ld = P.cnt["c0"]
                wcuse = [0, 0]
                wtcld = [0, 0]
                for mc in range(NCH):
                    @blk.sync
                    def _(s, mc=mc, pw=wcuse[mc % 2]):
                        P.wv(s, "pe", pw)
                        if mc == 0:
                            P.w(s, "dve")
                        sname = "c1" if mc % 2 == 0 else "qb"
                        P.inc(s.dma_start(out=wt_c[:, mc % 2],
                                          in_=wffnbf_r[:, :, 7 * 1024 + mc * 128:
                                                       7 * 1024 + (mc + 1) * 128]), sname, 16)
                    wtcld[mc % 2] = P.cnt["c1" if mc % 2 == 0 else "qb"]
                    with nc.psum_tensor(f"psC_uo_{mc}", [128, 256], F32) as pu:
                        @blk.tensor
                        def _(t, mc=mc, pu=pu, lw=wtcld[mc % 2]):
                            P.wv(t, "c1" if mc % 2 == 0 else "qb", lw)
                            P.w(t, "dve", "pool")
                            for ec in range(NCH):
                                i = t.matmul(pu[:], wt_c[:, mc % 2, ec, :], cbf[:, ec, :],
                                             start=(ec == 0), stop=(ec == NCH - 1))
                            P.inc(i, "pe")
                        wcuse[mc % 2] = P.cnt["pe"]
                        @blk.vector
                        def _(v, mc=mc, pu=pu, pw=wcuse[mc % 2]):
                            P.wv(v, "pe", pw)
                            v.tensor_copy(cuo[:, mc, :], pu[:])
                potmp = [P.cnt["dve"], P.cnt["dve"]]
                tnh = [P.cnt["act"], P.cnt["act"]]
                tmpw = [P.cnt["act"], P.cnt["act"]]
                ystore = [0]
                with nc.psum_tensor("psC_o0", [128, 256], F32) as po0, \
                     nc.psum_tensor("psC_o1", [128, 256], F32) as po1:
                    pos = (po0, po1)
                    for bt in range(B):
                        k = bt % 2
                        @blk.sync
                        def _(s, bt=bt, k=k, du=ysluse[k], sw=ypdone):
                            P.wv(s, "dve", du)
                            P.wv(s, "yp", sw)
                            if bt < 2:
                                P.w(s, "dve", "pool")
                            P.inc(s.dma_start(out=ysl[:, k],
                                              in_=ypark_r[:, :, bt * 256:(bt + 1) * 256]),
                                  f"y{k}", 16)
                        yslld[k] = P.cnt[f"y{k}"]
                        actb = P.cnt["act"]
                        dve_base = P.cnt["dve"]
                        @blk.scalar
                        def _(scg, actb=actb, dve_base=dve_base):
                            for mc in range(NCH):
                                P.wv(scg, "dve", dve_base + 2 * mc + 1)
                                scg.activation(tmp_c[:, mc % 2, :], tmp_c[:, mc % 2, :],
                                               AF.Tanh, bias=bv["bov"][:, mc:mc + 1])
                        assert P.cnt["act"] == actb + NCH
                        for mc in range(NCH):
                            m2 = mc % 2
                            po = pos[m2]
                            @blk.tensor
                            def _(t, mc=mc, bt=bt, po=po, lw=wo_ld, dw=potmp[m2]):
                                P.wv(t, "c0", lw)
                                P.wv(t, "dve", dw)
                                for ec in range(NCH):
                                    i = t.matmul(po[:], wo_res[:, ec, mc * 128:(mc + 1) * 128],
                                                 z[:, ec, bt * 256:(bt + 1) * 256],
                                                 start=(ec == 0), stop=(ec == NCH - 1))
                                P.inc(i, "pe")
                            och = P.cnt["pe"]
                            @blk.vector
                            def _(v, mc=mc, m2=m2, po=po, och=och,
                                  aw=actb + max(0, mc - 1)):
                                P.wv(v, "pe", och)
                                P.wv(v, "act", aw)
                                P.inc(v.tensor_add(tmp_c[:, m2, :], po[:], cuo[:, mc, :]),
                                      "dve")
                            potmp[m2] = P.cnt["dve"]
                            assert potmp[m2] == dve_base + 2 * mc + 1
                            @blk.vector
                            def _(v, mc=mc, m2=m2, k=k, aw=actb + mc + 1, lw=yslld[k],
                                  sw=ystore[0]):
                                P.wv(v, "act", aw)
                                P.wv(v, f"y{k}", lw)
                                P.wv(v, "rs", sw)
                                P.inc(v.tensor_add(ystage[:, mc, :], tmp_c[:, m2, :],
                                                   ysl[:, k, mc, :]), "dve")
                        ysluse[k] = P.cnt["dve"]
                        @blk.sync
                        def _(s, bt=bt, dw=P.cnt["dve"]):
                            P.wv(s, "dve", dw)
                            P.inc(s.dma_start(out=yo_r[:, :, bt * 256:(bt + 1) * 256],
                                              in_=ystage[:, :, :]), "rs", 16)
                        ystore[0] = P.cnt["rs"]
        @blk.sync
        def _(s):
            P.wall(s)
    return nc


_NC_CACHE = {}


def kernel(L0_attn_query, L0_attn_key, L0_attn_value, L0_attn_in_proj_weight,
           L0_attn_out_proj_weight, L0_ffn_wf, L0_ffn_uf, L0_ffn_bf, L0_ffn_wi,
           L0_ffn_ui, L0_ffn_bi, L0_ffn_wc, L0_ffn_uc, L0_ffn_bc, L0_ffn_wo,
           L0_ffn_uo, L0_ffn_bo, L0_ffn_in_c):
    x = np.asarray(L0_attn_query, np.float32)
    xf = np.ascontiguousarray(x.transpose(2, 0, 1).reshape(D, B * T))
    wqkv = np.ascontiguousarray(np.asarray(L0_attn_in_proj_weight, np.float32).T)
    wout = np.ascontiguousarray(np.asarray(L0_attn_out_proj_weight, np.float32).T)
    wmats = {k: np.ascontiguousarray(np.asarray(v, np.float32).T) for k, v in
             dict(wf=L0_ffn_wf, wi=L0_ffn_wi, wc=L0_ffn_wc, uf=L0_ffn_uf,
                  ui=L0_ffn_ui, uc=L0_ffn_uc, wo=L0_ffn_wo, uo=L0_ffn_uo).items()}
    bvecs = {k: np.ascontiguousarray(np.asarray(v, np.float32).reshape(NCH, 128).T)
             for k, v in dict(bfv=L0_ffn_bf, biv=L0_ffn_bi, bcv=L0_ffn_bc,
                              bov=L0_ffn_bo, incv=L0_ffn_in_c).items()}
    jj = np.arange(128)
    trimask = np.where(jj[:, None] > jj[None, :], np.float32(NEG), np.float32(0.0))
    trimask = np.ascontiguousarray(trimask.astype(np.float32))

    in_maps = []
    for core in range(8):
        blks = (core, 8 + core)
        xo = np.empty((D, OWN), np.float32)
        for b in range(B):
            for s_, blkid in enumerate(blks):
                xo[:, b * 256 + s_ * 128: b * 256 + (s_ + 1) * 128] = \
                    x[b, blkid * 128:(blkid + 1) * 128, :].T
        deny = np.zeros((128, 32), np.float32)
        for s_, blkid in enumerate(blks):
            for k in range(16):
                if k >= blkid:
                    deny[:, s_ * 16 + k] = NEG
        m = dict(xf=xf, xo=np.ascontiguousarray(xo), wqkv=wqkv, wout=wout,
                 deny=deny, trimask=trimask, iden=np.eye(128, dtype=np.float32))
        m.update(wmats)
        m.update(bvecs)
        in_maps.append(m)

    dbg = bool(_NC_CACHE.get("debug"))
    key = ("nc", dbg)
    if key not in _NC_CACHE:
        _NC_CACHE[key] = build_nc(debug=dbg)
    nc = _NC_CACHE[key]
    res = run_bass_kernel_spmd(nc, in_maps, core_ids=list(range(8)))

    if dbg:
        _NC_CACHE["results"] = res.results
    out = np.empty((B, T, D), np.float32)
    for core in range(8):
        yo = res.results[core]["yo"]
        for b in range(B):
            for s_, blkid in enumerate((core, 8 + core)):
                out[b, blkid * 128:(blkid + 1) * 128, :] = \
                    yo[:, b * 256 + s_ * 128: b * 256 + (s_ + 1) * 128].T
    return out
